# version 06
# baseline (speedup 1.0000x reference)
import numpy as np
import ml_dtypes
from contextlib import ExitStack

import concourse.bass as bass
import concourse.mybir as mybir
from concourse.bass_utils import run_bass_kernel_spmd
from concourse import library_config

F32 = mybir.dt.float32
BF16 = mybir.dt.bfloat16
I16 = mybir.dt.int16
U32 = mybir.dt.uint32
AF = mybir.ActivationFunctionType
ALU = mybir.AluOpType
AX = mybir.AxisListType

B, T, C, HDIM, E = 4, 4096, 1024, 4096, 8
HALF = T // 2
NCH = T // 128
OWN0 = NCH // 2
CAP = 640
DUMMY = HALF
KSEL = 1024
EPS = 1e-5
NBLK = CAP // 128
BISECT_ITERS = 32

BF = ml_dtypes.bfloat16


class Sem:
    def __init__(self, nc, stack, name):
        self.h = stack.enter_context(nc.semaphore(name))
        self.n = 0

    def inc(self, instr, k=1):
        instr.then_inc(self.h, k)
        self.n += k
        return self.n


def build_program(phases=4, dbg=False, nch=NCH, ysub=6, serial=False, skip=()):
    nc = bass.Bass("TRN2", target_bir_lowering=False)

    d_z = nc.dram_tensor("zc", [T, C], F32, kind="ExternalInput").ap()
    d_zt = nc.dram_tensor("zct", [C, T], F32, kind="ExternalInput").ap()
    d_kap = nc.dram_tensor("kap", [128, 1], F32, kind="ExternalInput").ap()
    d_dec = nc.dram_tensor("dec", [1, 1], F32, kind="ExternalInput").ap()
    d_gw = nc.dram_tensor("gw", [C, E], F32, kind="ExternalInput").ap()
    d_wt = nc.dram_tensor("wt", [C, C], F32, kind="ExternalInput").ap()
    d_batt = nc.dram_tensor("batt", [1, C], F32, kind="ExternalInput").ap()
    d_gam = nc.dram_tensor("gam", [1, C], F32, kind="ExternalInput").ap()
    d_bet = nc.dram_tensor("bet", [1, C], F32, kind="ExternalInput").ap()
    d_w1 = nc.dram_tensor("w1r", [E, 4, 128, 8, 1024], F32, kind="ExternalInput").ap()
    d_w2 = nc.dram_tensor("w2r", [E, 4, 128, 8, 1024], F32, kind="ExternalInput").ap()
    d_expn = nc.dram_tensor("expn", [128, 128], F32, kind="ExternalInput").ap()
    d_mask = nc.dram_tensor("maskc", [128, 128], F32, kind="ExternalInput").ap()
    d_tp1 = nc.dram_tensor("tp1", [128, 128], F32, kind="ExternalInput").ap()
    d_idxc = nc.dram_tensor("idxc", [128, 128], F32, kind="ExternalInput").ap()

    d_xn = nc.dram_tensor("xn_s", [HALF + 1, C], BF16, kind="ExternalOutput" if dbg else "Internal").ap()
    d_pr = nc.dram_tensor("pr_s", [HALF + 1, 64], F32, kind="ExternalOutput" if dbg else "Internal").ap()
    d_s = nc.dram_tensor("s_s", [HALF, C], F32, kind="ExternalOutput" if dbg else "Internal").ap()
    d_acc = nc.dram_tensor("acc_s", [HALF + 1, C], F32, kind="ExternalOutput" if dbg else "Internal").ap()
    d_out = nc.dram_tensor("out", [HALF, C], F32, kind="ExternalOutput").ap()
    if dbg:
        d_dbgprobs = nc.dram_tensor("dbgprobs", [128, NCH * E], F32, kind="ExternalOutput").ap()
        d_dbgtau = nc.dram_tensor("dbgtau", [128, E], F32, kind="ExternalOutput").ap()
        d_dbgidx = nc.dram_tensor("dbgidx", [128, E, CAP // 16], I16, kind="ExternalOutput").ap()
        d_dbgnf = nc.dram_tensor("dbgnf", [1, E], U32, kind="ExternalOutput").ap()
        d_dbgu = nc.dram_tensor("dbgu", [128, 128], BF16, kind="ExternalOutput").ap()
        d_dbga = nc.dram_tensor("dbga", [128, 128], F32, kind="ExternalOutput").ap()
        d_dbgbi = nc.dram_tensor("dbgbi", [128, 4 * E], F32, kind="ExternalOutput").ap()

    with ExitStack() as ctx:
        en = ctx.enter_context

        probs = en(nc.sbuf_tensor("probs", [128, NCH * E], F32))
        Ub = en(nc.sbuf_tensor("Ub", [128, 128], BF16))
        apow = en(nc.sbuf_tensor("apow", [128, 128], F32))
        kap = en(nc.sbuf_tensor("kapt", [128, 1], F32))
        zcol = en(nc.sbuf_tensor("zcol", [128, 1], F32))
        ones128b = en(nc.sbuf_tensor("ones128b", [128, 1], BF16))
        zxnb = en(nc.sbuf_tensor("zxnb", [1, C], BF16))
        onesm = en(nc.sbuf_tensor("onesm", [128, 128], F32))
        sgin = en(nc.sbuf_tensor("sgin", [16, E, 168], I16))
        sgout = en(nc.sbuf_tensor("sgout", [16, E, 168], I16))
        idxall = en(nc.sbuf_tensor("idxall", [128, E, CAP // 16], I16))
        nfb = en(nc.sbuf_tensor("nfb", [1, E], U32))
        taub = en(nc.sbuf_tensor("taub", [128, E], F32))
        bdbg = en(nc.sbuf_tensor("bdbg", [128, 4 * E], F32))
        idxc = en(nc.sbuf_tensor("idxcs", [128, 128], F32))
        w1s = [en(nc.sbuf_tensor(f"w1s{i}", [128, 8, 1024], BF16)) for i in range(2)]
        w2s = en(nc.sbuf_tensor("w2s", [128, 32, 1024], BF16))

        s_in = Sem(nc, ctx, "s_in")
        s_gw = Sem(nc, ctx, "s_gw")
        s_zc = [Sem(nc, ctx, f"s_zc{i}") for i in range(3)]
        s_zt = [Sem(nc, ctx, f"s_zt{i}") for i in range(3)]
        s_wt = Sem(nc, ctx, "s_wt")
        s_acc0 = Sem(nc, ctx, "s_acc0")
        s_sw = [Sem(nc, ctx, f"s_sw{i}") for i in range(2)]
        s_xw = [Sem(nc, ctx, f"s_xw{i}") for i in range(3)]
        s_pw = Sem(nc, ctx, "s_pw")
        s_w1 = [Sem(nc, ctx, f"s_w1{i}") for i in range(2)]
        s_w2 = Sem(nc, ctx, "s_w2")
        s_gath = Sem(nc, ctx, "s_gath")
        s_scat = Sem(nc, ctx, "s_scat")
        s_ta = [Sem(nc, ctx, f"s_ta{i}") for i in range(4)]
        s_tg = [Sem(nc, ctx, f"s_tg{i}") for i in range(4)]
        s_st = [Sem(nc, ctx, f"s_st{i}") for i in range(4)]
        sgp = Sem(nc, ctx, "sgp")
        sv = Sem(nc, ctx, "sv")
        ssc = Sem(nc, ctx, "ssc")
        spe = Sem(nc, ctx, "spe")

        block = en(nc.Block())
        _allsems = ([s_in, s_gw, s_wt, s_acc0, s_pw, s_w2, s_gath, s_scat,
                     sgp, sv, ssc, spe]
                    + s_zc + s_zt + s_sw + s_xw + s_w1 + s_ta + s_tg + s_st)

        def _wrap(emitter):
            def outer(f):
                if not serial:
                    return emitter(f)

                def g(eng):
                    for _s in _allsems:
                        if _s.n:
                            eng.wait_ge(_s.h, _s.n)
                    return f(eng)
                return emitter(g)
            return outer

        SY, V, SC, PE, G = (_wrap(block.sync), _wrap(block.vector),
                            _wrap(block.scalar), _wrap(block.tensor),
                            _wrap(block.gpsimd))

        p0 = ExitStack()
        e0 = p0.enter_context
        cexpn = e0(nc.sbuf_tensor("cexpn", [128, 128], F32))
        cmask = e0(nc.sbuf_tensor("cmask", [128, 128], F32))
        ctp1 = e0(nc.sbuf_tensor("ctp1", [128, 128], F32))
        dect = e0(nc.sbuf_tensor("dect", [128, 1], F32))
        lnab = e0(nc.sbuf_tensor("lnab", [128, 1], F32))
        uexp = e0(nc.sbuf_tensor("uexp", [128, 128], F32))

        def _(sync):
            for dst, src in [(cexpn, d_expn), (cmask, d_mask), (ctp1, d_tp1),
                             (kap, d_kap), (idxc, d_idxc)]:
                s_in.inc(sync.dma_start(out=dst[:], in_=src), 16)
            s_in.inc(sync.dma_start(out=dect[:], in_=d_dec.partition_broadcast(128)), 16)
        SY(_)
        p0_loads = s_in.n

        def _(v):
            v.memset(zcol[:], 0.0)
            v.memset(zxnb[:], 0.0)
            v.memset(ones128b[:], 1.0)
            v.memset(onesm[:], 1.0)
            v.memset(sgin[:, :, 128:168], DUMMY)
            v.memset(idxall[:], 0)
            if dbg:
                v.memset(bdbg[:], 0.0)
                v.memset(taub[:], 0.0)
                v.memset(nfb[:], 0)
            sv.inc(v.engine_nop())
        V(_)
        v_init = sv.n

        def _(s):
            s.wait_ge(s_in.h, p0_loads)
            ssc.inc(s.activation(lnab[:], dect[:], AF.Sigmoid))
            s.wait_ge(ssc.h, ssc.n)
            ssc.inc(s.activation(lnab[:], lnab[:], AF.Ln))
        SC(_)
        sc_lna = ssc.n

        def _(v):
            v.wait_ge(ssc.h, sc_lna)
            sv.inc(v.tensor_scalar(uexp[:], cexpn[:], lnab[:, 0:1], None, ALU.mult))
        V(_)
        v_uexp = sv.n

        def _(s):
            s.wait_ge(sv.h, v_uexp)
            ssc.inc(s.activation(uexp[:], uexp[:], AF.Exp))
        SC(_)
        sc_uexp = ssc.n

        def _(v):
            v.wait_ge(ssc.h, sc_uexp)
            sv.inc(v.tensor_tensor(Ub[:], uexp[:], cmask[:], ALU.mult))
            v.wait_ge(sv.h, sv.n)
            sv.inc(v.tensor_scalar(uexp[:], ctp1[:], lnab[:, 0:1], None, ALU.mult))
        V(_)
        v_ap1 = sv.n

        def _(s):
            s.wait_ge(sv.h, v_ap1)
            ssc.inc(s.activation(apow[:], uexp[:], AF.Exp))
        SC(_)
        sc_apow = ssc.n
        p0.close()

        p1 = ExitStack()
        e1 = p1.enter_context
        zero = e1(nc.sbuf_tensor("zerot", [128, C], F32))
        ZBUF = 3
        zc = [e1(nc.sbuf_tensor(f"zct{i}", [128, C], F32)) for i in range(ZBUF)]
        ztc = [e1(nc.sbuf_tensor(f"ztct{i}", [128, 8, 128], F32)) for i in range(3)]
        sq = e1(nc.sbuf_tensor("sq", [128, C], F32))
        xnb = [e1(nc.sbuf_tensor(f"xnb{i}", [128, C], BF16)) for i in range(3)]
        sfm = [e1(nc.sbuf_tensor(f"sfm{i}", [128, 8, 128], BF16)) for i in range(3)]
        carrk = e1(nc.sbuf_tensor("carrk", [128, 8], F32))
        yb = [e1(nc.sbuf_tensor(f"yb{i}", [128, C], F32)) for i in range(2)]
        ynb = [e1(nc.sbuf_tensor(f"ynbt{i}", [128, C], BF16)) for i in range(2)]
        stile = [e1(nc.sbuf_tensor(f"stile{i}", [128, C], F32)) for i in range(2)]
        wtb = e1(nc.sbuf_tensor("wtb", [128, 8, C], BF16))
        gws = e1(nc.sbuf_tensor("gws", [128, 8, E], F32))
        csb = e1(nc.sbuf_tensor("csb", [128, E], F32))
        cspart = e1(nc.sbuf_tensor("cspart", [128, E], F32))
        batb = e1(nc.sbuf_tensor("batb", [128, C], F32))
        gamb = e1(nc.sbuf_tensor("gamb", [128, C], BF16))
        betb = e1(nc.sbuf_tensor("betb", [128, C], BF16))

        def st2(name):
            return e1(nc.sbuf_tensor(name, [128, 2], F32))

        zsum, ssq, meant, vart, rstd, nrstd, nbias = (st2(n) for n in
            ("zsum", "ssq", "meant", "vart", "rstdt", "nrstdt", "nbiast"))
        rmax, rsum, rcp, t128a = (st2(n) for n in ("rmaxt", "rsumt", "rcpt", "t128a"))
        ysum, yssq, ymean, yvar, yrstd, ynbias, yt = (st2(n) for n in
            ("ysumt", "yssqt", "ymeant", "yvart", "yrstdt", "ynbiast", "ytt"))
        scoret = e1(nc.sbuf_tensor("scoret", [128, 2, E], F32))
        expt = e1(nc.sbuf_tensor("exptt", [128, 2, E], F32))

        gate_ps = [e1(nc.psum_tensor(f"gateps{i}", [128, E], F32)) for i in range(2)]
        scan_ps = [e1(nc.psum_tensor(f"scanps{i}", [128, 8, 128], F32)) for i in range(2)]
        y_ps = e1(nc.psum_tensor("yps", [128, C], F32))
        b_ps = y_ps

        def _(v):
            v.wait_ge(ssc.h, sc_apow)
            sv.inc(v.memset(zero[:], 0.0))
        V(_)
        v_zero = sv.n

        def _(sync):
            sync.wait_ge(sv.h, v_zero)
            sync.wait_ge(sv.h, v_init)
            for i in range(17):
                rows = min(128, HALF + 1 - i * 128)
                s_acc0.inc(sync.dma_start(out=d_acc[i * 128:i * 128 + rows, :],
                                          in_=zero[0:rows, :]), 16)
            s_acc0.inc(sync.dma_start(out=d_xn[DUMMY:DUMMY + 1, :], in_=zxnb[:]), 16)
            s_acc0.inc(sync.dma_start(out=d_pr[DUMMY:DUMMY + 1, :], in_=zero[0:1, 0:64]), 16)
            for i in range(16):
                s_acc0.inc(sync.dma_start(out=d_pr[i * 128:(i + 1) * 128, E:64],
                                          in_=zero[0:128, 0:64 - E]), 16)
        SY(_)
        acc0_done = s_acc0.n

        def _(sync):
            sync.wait_ge(ssc.h, sc_apow)
            s_gw.inc(sync.dma_start(out=gws[:], in_=d_gw.rearrange("(cc p) e -> p cc e", p=128)), 16)
            s_gw.inc(sync.dma_start(out=batb[:], in_=d_batt.partition_broadcast(128)), 16)
        SY(_)
        gw_loaded = s_gw.n

        def _(g):
            g.wait_ge(sv.h, v_init)
            g.wait_ge(ssc.h, sc_apow)
            s_wt.inc(g.dma_start(out=wtb[:], in_=d_wt.rearrange("(dc p) e -> p dc e", p=128)), 16)
            s_wt.inc(g.dma_start(out=gamb[:], in_=d_gam.partition_broadcast(128)), 16)
            s_wt.inc(g.dma_start(out=betb[:], in_=d_bet.partition_broadcast(128)), 16)
        G(_)
        wtb_loaded = s_wt.n

        def _(v):
            v.wait_ge(s_gw.h, gw_loaded)
            v.wait_ge(sv.h, v_init)
            sv.inc(v.reduce_sum(cspart[:], gws[:].rearrange("p cc e -> p e cc"), axis=AX.X))
        V(_)
        v_cspart = sv.n

        def _(t):
            t.wait_ge(sv.h, v_cspart)
            spe.inc(t.matmul(b_ps[:, 0:E], onesm[:], cspart[:], start=True, stop=True))
        PE(_)
        pe_csb = spe.n

        def _(v):
            v.wait_ge(spe.h, pe_csb)
            sv.inc(v.tensor_copy(csb[:], b_ps[:, 0:E]))
        V(_)
        v_bcast = sv.n

        zc_load = [0] * NCH
        ztc_load = [0] * NCH
        v_var = [0] * NCH
        v_stats = [0] * NCH
        v_sm1 = [0] * NCH
        v_probs = [0] * NCH
        v_scan = [0] * NCH
        v_yevac = [0] * NCH
        v_yvar = [0] * NCH
        v_ynb = [0] * NCH
        v_s = [0] * NCH
        pe_gate = [0] * NCH
        pe_scan = [0] * NCH
        pe_y = [0] * NCH
        dma_sxnp = [0] * NCH
        dma_xwp = [0] * NCH

        sscp_ss = [0] * NCH
        sscp_rstd = [0] * NCH
        sscp_cast = [0] * NCH
        sscp_exp = [0] * NCH
        sscp_yss = [0] * NCH
        sscp_yrstd = [0] * NCH
        sscp_yn = [0] * NCH
        _c = ssc.n
        for m in range(nch):
            if m - 2 >= OWN0:
                _c += 1; sscp_yss[m - 2] = _c
                _c += 2; sscp_yrstd[m - 2] = _c
                _c += 1; sscp_yn[m - 2] = _c
            _c += 1; sscp_ss[m] = _c
            _c += 2; sscp_rstd[m] = _c
            _c += 1; sscp_cast[m] = _c
            _c += 1; sscp_exp[m] = _c
        for mm_ in range(max(OWN0, nch - 2), nch):
            _c += 1; sscp_yss[mm_] = _c
            _c += 2; sscp_yrstd[mm_] = _c
            _c += 1; sscp_yn[mm_] = _c

        def emit_yside(s, mm, barrier):
            pam = mm % 2
            ybp, ynp = yb[pam], ynb[pam]
            s.wait_ge(sv.h, v_yevac[mm])
            s.wait_ge(ssc.h, barrier)
            _n = ssc.inc(s.activation(sq[:], ybp[:], AF.Square,
                                      accum_out=yssq[:, pam:pam + 1]))
            assert _n == sscp_yss[mm], (mm, _n)
            s.wait_ge(sv.h, v_yvar[mm])
            ssc.inc(s.activation(yrstd[:, pam:pam + 1], yvar[:, pam:pam + 1], AF.Ln))
            s.wait_ge(ssc.h, ssc.n)
            _n = ssc.inc(s.activation(yrstd[:, pam:pam + 1], yrstd[:, pam:pam + 1],
                                      AF.Exp, scale=-0.5))
            assert _n == sscp_yrstd[mm], (mm, _n)
            s.wait_ge(sv.h, v_ynb[mm])
            if mm - 2 >= OWN0:
                s.wait_ge(sv.h, v_s[mm - 2])
            _n = ssc.inc(s.activation(ynp[:], ybp[:], AF.Identity,
                                      bias=ynbias[:, pam:pam + 1],
                                      scale=yrstd[:, pam:pam + 1]))
            assert _n == sscp_yn[mm], (mm, _n)

        def emit_yvec(my):
            pam = my % 2
            yb_t, yn_t, st_t = yb[pam], ynb[pam], stile[pam]
            xby = xnb[my % 3]

            def _(v, my=my, yb_t=yb_t, pam=pam):
                v.wait_ge(spe.h, pe_y[my])
                if my >= OWN0 + 2:
                    v.wait_ge(ssc.h, sscp_yn[my - 2])
                sv.inc(v.tensor_tensor(yb_t[:], y_ps[:], batb[:], ALU.add))
                v.wait_ge(sv.h, sv.n)
                sv.inc(v.reduce_sum(ysum[:, pam:pam + 1], yb_t[:], axis=AX.X))
                v.wait_ge(sv.h, sv.n)
                sv.inc(v.tensor_scalar(ymean[:, pam:pam + 1], ysum[:, pam:pam + 1],
                                       1.0 / C, None, ALU.mult))
            V(_)
            v_yevac[my] = sv.n

            def _(v, my=my, pam=pam):
                v.wait_ge(ssc.h, sscp_yss[my])
                v.tensor_scalar(yvar[:, pam:pam + 1], yssq[:, pam:pam + 1],
                                1.0 / C, EPS, ALU.mult, ALU.add)
                sv.inc(v.tensor_tensor(yt[:, pam:pam + 1], ymean[:, pam:pam + 1],
                                       ymean[:, pam:pam + 1], ALU.mult))
                v.wait_ge(sv.h, sv.n)
                sv.inc(v.tensor_sub(yvar[:, pam:pam + 1], yvar[:, pam:pam + 1],
                                    yt[:, pam:pam + 1]))
            V(_)
            v_yvar[my] = sv.n

            def _(v, my=my, pam=pam):
                v.wait_ge(ssc.h, sscp_yrstd[my])
                sv.inc(v.tensor_scalar(yt[:, pam:pam + 1], yrstd[:, pam:pam + 1],
                                       -1.0, None, ALU.mult))
                v.wait_ge(sv.h, sv.n)
                sv.inc(v.tensor_tensor(ynbias[:, pam:pam + 1], ymean[:, pam:pam + 1],
                                       yt[:, pam:pam + 1], ALU.mult))
            V(_)
            v_ynb[my] = sv.n

            def _(v, my=my, yn_t=yn_t, st_t=st_t, pam=pam):
                v.wait_ge(ssc.h, sscp_yn[my])
                v.wait_ge(s_wt.h, wtb_loaded)
                if my >= OWN0 + 2:
                    v.wait_ge(s_sw[pam].h, dma_sxnp[my - 2])
                sv.inc(v.tensor_tensor(st_t[:], yn_t[:], gamb[:], ALU.mult))
                v.wait_ge(sv.h, sv.n)
                sv.inc(v.tensor_tensor(st_t[:], st_t[:], betb[:], ALU.add))
            V(_)
            v_s[my] = sv.n

            def _(g, my=my, st_t=st_t, xby=xby, pam=pam):
                r0 = (my - OWN0) * 128
                g.wait_ge(sv.h, v_s[my])
                s_sw[pam].inc(g.dma_start(out=d_s[r0:r0 + 128, :], in_=st_t[:]), 16)
                dma_sxnp[my] = s_sw[pam].n
                g.wait_ge(ssc.h, sscp_cast[my])
                s_xw[my % 3].inc(g.dma_start(out=d_xn[r0:r0 + 128, :], in_=xby[:]), 16)
                dma_xwp[my] = s_xw[my % 3].n
                g.wait_ge(sv.h, v_probs[my])
                s_pw.inc(g.dma_start(out=d_pr[r0:r0 + 128, 0:E],
                                     in_=probs[:, my * E:(my + 1) * E]), 16)
            G(_)

        def _(sync):
            for mm_ in range(min(2, nch)):
                sync.wait_ge(ssc.h, sc_apow)
                s_zt[mm_ % 3].inc(sync.dma_start(
                    out=ztc[mm_ % 3][:],
                    in_=d_zt[:, mm_ * 128:(mm_ + 1) * 128].rearrange(
                        "(cc p) t -> p cc t", p=128)), 16)
                ztc_load[mm_] = s_zt[mm_ % 3].n
        SY(_)

        for m in range(nch):
            par = m % 2
            zb, tb, xb, sf = zc[m % ZBUF], ztc[m % 3], xnb[m % 3], sfm[m % 3]
            gp, sp = gate_ps[par], scan_ps[par]

            def _(sync, m=m, zb=zb):
                if m >= 3:
                    sync.wait_ge(ssc.h, sscp_cast[m - 3])
                s_zc[m % 3].inc(sync.dma_start(out=zb[:], in_=d_z[m * 128:(m + 1) * 128, :]), 16)
                zc_load[m] = s_zc[m % 3].n
            SY(_)

            def _(v, m=m, zb=zb, par=par):
                v.wait_ge(s_zc[m % 3].h, zc_load[m])
                if m >= 2:
                    v.wait_ge(ssc.h, sscp_cast[m - 2])
                v.reduce_sum(zsum[:, par:par + 1], zb[:], axis=AX.X)
                v.wait_ge(ssc.h, sscp_ss[m])
                sv.inc(v.tensor_scalar(vart[:, par:par + 1], ssq[:, par:par + 1],
                                       1.0 / C, EPS, ALU.mult, ALU.add))
                v.wait_ge(sv.h, sv.n)
                sv.inc(v.tensor_scalar(meant[:, par:par + 1], zsum[:, par:par + 1],
                                       1.0 / C, None, ALU.mult))
                v.wait_ge(sv.h, sv.n)
                sv.inc(v.tensor_tensor(t128a[:, par:par + 1], meant[:, par:par + 1],
                                       meant[:, par:par + 1], ALU.mult))
                v.wait_ge(sv.h, sv.n)
                sv.inc(v.tensor_sub(vart[:, par:par + 1], vart[:, par:par + 1],
                                    t128a[:, par:par + 1]))
            V(_)
            v_var[m] = sv.n

            def _(v, m=m, par=par):
                v.wait_ge(ssc.h, sscp_rstd[m])
                if m >= 2:
                    v.wait_ge(ssc.h, sscp_cast[m - 2])
                sv.inc(v.tensor_scalar(nrstd[:, par:par + 1], rstd[:, par:par + 1],
                                       -1.0, None, ALU.mult))
                v.wait_ge(sv.h, sv.n)
                sv.inc(v.tensor_tensor(nbias[:, par:par + 1], meant[:, par:par + 1],
                                       nrstd[:, par:par + 1], ALU.mult))
            V(_)
            v_stats[m] = sv.n

            def _(t, m=m, tb=tb, gp=gp):
                t.wait_ge(s_zt[m % 3].h, ztc_load[m])
                t.wait_ge(s_gw.h, gw_loaded)
                if m >= 2:
                    t.wait_ge(sv.h, v_sm1[m - 2])
                for cc in range(8):
                    mm = t.matmul(gp[:, 0:E], tb[:, cc, :], gws[:, cc, :],
                                  start=(cc == 0), stop=(cc == 7))
                spe.inc(mm)
            PE(_)
            pe_gate[m] = spe.n

            def _(v, m=m, gp=gp, par=par):
                v.wait_ge(spe.h, pe_gate[m])
                if m == 0:
                    v.wait_ge(sv.h, v_bcast)
                if m >= 2:
                    v.wait_ge(ssc.h, sscp_exp[m - 2])
                sv.inc(v.scalar_tensor_tensor(scoret[:, par, :], csb[:], meant[:, par:par + 1],
                                              gp[:, 0:E], ALU.mult, ALU.subtract))
                v.wait_ge(sv.h, sv.n)
                sv.inc(v.tensor_scalar(scoret[:, par, :], scoret[:, par, :],
                                       nrstd[:, par:par + 1], None, ALU.mult))
                v.wait_ge(sv.h, sv.n)
                sv.inc(v.reduce_max(rmax[:, par:par + 1], scoret[:, par, :], axis=AX.X))
                v.wait_ge(sv.h, sv.n)
                sv.inc(v.tensor_scalar(rmax[:, par:par + 1], rmax[:, par:par + 1],
                                       -1.0, None, ALU.mult))
            V(_)
            v_sm1[m] = sv.n

            def _(v, m=m, par=par):
                v.wait_ge(ssc.h, sscp_exp[m])
                sv.inc(v.reduce_sum(rsum[:, par:par + 1], expt[:, par, :], axis=AX.X))
                v.wait_ge(sv.h, sv.n)
                sv.inc(v.reciprocal(rcp[:, par:par + 1], rsum[:, par:par + 1]))
                v.wait_ge(sv.h, sv.n)
                sv.inc(v.tensor_scalar(probs[:, m * E:(m + 1) * E], expt[:, par, :],
                                       rcp[:, par:par + 1], None, ALU.mult))
            V(_)
            v_probs[m] = sv.n

            def _(t, m=m, xb=xb, sp=sp):
                t.wait_ge(ssc.h, sscp_cast[m])
                t.wait_ge(ssc.h, sc_apow)
                if m >= 2:
                    t.wait_ge(sv.h, v_scan[m - 2])
                for dc in range(8):
                    mm = t.matmul(sp[:, dc, :], xb[:, dc * 128:(dc + 1) * 128], Ub[:],
                                  start=True, stop=True)
                spe.inc(mm)
            PE(_)
            pe_scan[m] = spe.n

            def _(v, m=m, sp=sp, sf=sf):
                v.wait_ge(spe.h, pe_scan[m])
                if m >= 3 and m - 3 >= OWN0:
                    v.wait_ge(spe.h, pe_y[m - 3])
                if m == 0:
                    for dc in range(8):
                        mmv = v.tensor_copy(carrk[:, dc:dc + 1], zcol[:])
                else:
                    prev = sfm[(m - 1) % 3]
                    for dc in range(8):
                        mmv = v.tensor_copy(carrk[:, dc:dc + 1], prev[:, dc, 127:128])
                    if m == OWN0:
                        sv.inc(mmv)
                        v.wait_ge(sv.h, sv.n)
                        for dc in range(8):
                            mmv = v.tensor_tensor(carrk[:, dc:dc + 1], carrk[:, dc:dc + 1],
                                                  kap[:], ALU.mult)
                sv.inc(mmv)
                v.wait_ge(sv.h, sv.n)
                for dc in range(8):
                    last = v.scalar_tensor_tensor(sf[:, dc, :], apow[:], carrk[:, dc:dc + 1],
                                                  sp[:, dc, :], ALU.mult, ALU.add)
                sv.inc(last)
            V(_)
            v_scan[m] = sv.n

            if m - 1 >= OWN0:
                emit_yvec(m - 1)

            if m >= OWN0:
                def _(t, m=m, sf=sf):
                    t.wait_ge(sv.h, v_scan[m])
                    if m == OWN0:
                        t.wait_ge(s_wt.h, wtb_loaded)
                    if m > OWN0:
                        t.wait_ge(sv.h, v_yevac[m - 1])
                    for dc in range(8):
                        t.matmul(y_ps[:, 0:512], sf[:, dc, :], wtb[:, dc, 0:512],
                                 start=(dc == 0), stop=(dc == 7))
                        mm = t.matmul(y_ps[:, 512:1024], sf[:, dc, :], wtb[:, dc, 512:1024],
                                      start=(dc == 0), stop=(dc == 7))
                    spe.inc(mm)
                PE(_)
                pe_y[m] = spe.n

            def _(s, m=m, zb=zb, xb=xb, par=par):
                if m - 2 >= OWN0:
                    emit_yside(s, m - 2, sscp_exp[m - 1])
                s.wait_ge(s_zc[m % 3].h, zc_load[m])
                if m >= 2:
                    s.wait_ge(sv.h, v_var[m - 2])
                if m - 2 >= OWN0:
                    s.wait_ge(ssc.h, sscp_yn[m - 2])
                elif m >= 1:
                    s.wait_ge(ssc.h, sscp_exp[m - 1])
                _n = ssc.inc(s.activation(sq[:], zb[:], AF.Square,
                                          accum_out=ssq[:, par:par + 1]))
                assert _n == sscp_ss[m], (m, _n)
                s.wait_ge(sv.h, v_var[m])
                ssc.inc(s.activation(rstd[:, par:par + 1], vart[:, par:par + 1], AF.Ln))
                s.wait_ge(ssc.h, ssc.n)
                _n = ssc.inc(s.activation(rstd[:, par:par + 1], rstd[:, par:par + 1],
                                          AF.Exp, scale=-0.5))
                assert _n == sscp_rstd[m], (m, _n)
                s.wait_ge(sv.h, v_stats[m])
                if m >= 3:
                    s.wait_ge(spe.h, pe_scan[m - 3])
                    if m - 3 >= OWN0:
                        s.wait_ge(s_xw[m % 3].h, dma_xwp[m - 3])
                _n = ssc.inc(s.activation(xb[:], zb[:], AF.Identity,
                                          bias=nbias[:, par:par + 1],
                                          scale=rstd[:, par:par + 1]))
                assert _n == sscp_cast[m], (m, _n)
                s.wait_ge(sv.h, v_sm1[m])
                _n = ssc.inc(s.activation(expt[:, par, :], scoret[:, par, :], AF.Exp,
                                          bias=rmax[:, par:par + 1]))
                assert _n == sscp_exp[m], (m, _n)
                if m + 2 < nch:
                    if m >= 1:
                        s.wait_ge(spe.h, pe_gate[m - 1])
                    s_zt[(m + 2) % 3].inc(s.dma_start(
                        out=ztc[(m + 2) % 3][:],
                        in_=d_zt[:, (m + 2) * 128:(m + 3) * 128].rearrange(
                            "(cc p) t -> p cc t", p=128)), 16)
                    ztc_load[m + 2] = s_zt[(m + 2) % 3].n
            SC(_)

        if nch - 1 >= OWN0:
            emit_yvec(nch - 1)

            def _(s):
                for mm_ in range(max(OWN0, nch - 2), nch):
                    emit_yside(s, mm_, ssc.n)
            SC(_)
        assert ssc.n == _c, (ssc.n, _c)

        p1_v_done = sv.n
        p1_sc_done = ssc.n
        p1_pe_done = spe.n
        xn_wr_snap = [(_sem, _sem.n) for _sem in s_xw + s_sw + [s_pw]]
        p1.close()

        if phases >= 2:
            p2 = ExitStack()
            e2 = p2.enter_context
            lo = e2(nc.sbuf_tensor("lot", [128, E], F32))
            hi = e2(nc.sbuf_tensor("hit", [128, E], F32))
            mid = e2(nc.sbuf_tensor("midt", [128, E], F32))
            dec2 = e2(nc.sbuf_tensor("dec2", [128, E], F32))
            td1 = e2(nc.sbuf_tensor("td1", [128, E], F32))
            td2 = e2(nc.sbuf_tensor("td2", [128, E], F32))
            ind = e2(nc.sbuf_tensor("indt", [128, NCH, E], BF16))
            partb = e2(nc.sbuf_tensor("partb", [128, E], F32))
            msk = e2(nc.sbuf_tensor("mskt", [128, 128], F32))
            idv = e2(nc.sbuf_tensor("idvt", [128, 128], F32))
            cnt_ps = e2(nc.psum_tensor("cntps", [128, E], F32))

            def _(v):
                v.wait_ge(sv.h, p1_v_done)
                v.wait_ge(ssc.h, p1_sc_done)
                for _sem, _val in xn_wr_snap:
                    v.wait_ge(_sem.h, _val)
                v.wait_ge(spe.h, p1_pe_done)
                for _sem in [s_in, s_gw, s_wt, s_acc0] + s_zc + s_zt:
                    v.wait_ge(_sem.h, _sem.n)
                if dbg:
                    sv.inc(v.tensor_copy(bdbg[:, 2 * E:3 * E], probs[:, 0:E]))
                v.memset(lo[:], 0.0)
                v.memset(hi[:], 1.1)
                sv.inc(v.memset(mid[:], 0.55))
            V(_)
            v_tb = sv.n

            for it in range(BISECT_ITERS):
                def _(v, v_tb=v_tb):
                    v.wait_ge(sv.h, v_tb)
                    sv.inc(v.tensor_tensor(
                        ind[:],
                        probs[:].rearrange("p (m e) -> p m e", e=E),
                        mid[:].unsqueeze(1).to_broadcast([128, NCH, E]),
                        ALU.is_ge))
                    v.wait_ge(sv.h, sv.n)
                    sv.inc(v.reduce_sum(partb[:], ind[:].rearrange("p m e -> p e m"),
                                        axis=AX.X))
                V(_)
                v_cmp = sv.n

                def _(t, v_cmp=v_cmp):
                    t.wait_ge(sv.h, v_cmp)
                    spe.inc(t.matmul(cnt_ps[:], onesm[:], partb[:], start=True, stop=True))
                PE(_)
                pe_cnt = spe.n

                def _(v, pe_cnt=pe_cnt):
                    v.wait_ge(spe.h, pe_cnt)
                    v.tensor_sub(td1[:], mid[:], lo[:])
                    v.tensor_sub(td2[:], hi[:], mid[:])
                    sv.inc(v.tensor_scalar(dec2[:], cnt_ps[:], float(KSEL), None, ALU.is_ge))
                    v.wait_ge(sv.h, sv.n)
                    v.tensor_tensor(td1[:], td1[:], dec2[:], ALU.mult)
                    sv.inc(v.tensor_tensor(td2[:], td2[:], dec2[:], ALU.mult))
                    v.wait_ge(sv.h, sv.n)
                    v.tensor_add(lo[:], lo[:], td1[:])
                    sv.inc(v.tensor_add(hi[:], mid[:], td2[:]))
                    v.wait_ge(sv.h, sv.n)
                    sv.inc(v.tensor_add(td1[:], lo[:], hi[:]))
                    v.wait_ge(sv.h, sv.n)
                    sv.inc(v.tensor_scalar(mid[:], td1[:], 0.5, None, ALU.mult))
                V(_)
                v_tb = sv.n

            def _(v, v_tb=v_tb):
                v.wait_ge(sv.h, v_tb)
                v.tensor_copy(taub[:], lo[:])
                sv.inc(v.tensor_tensor(
                    msk[:].rearrange("p (e m) -> p e m", e=E),
                    probs[:, OWN0 * E:NCH * E].rearrange("p (m e) -> p e m", e=E),
                    lo[:].unsqueeze(2).to_broadcast([128, E, OWN0]),
                    ALU.is_ge))
                v.wait_ge(sv.h, sv.n)
                sv.inc(v.tensor_tensor(idv[:], msk[:], idxc[:], ALU.mult))
                v.wait_ge(sv.h, sv.n)
                sv.inc(v.tensor_scalar(idv[:], idv[:], -1.0, None, ALU.add))
            V(_)
            v_sel = sv.n

            def _(g):
                g.wait_ge(sv.h, v_sel)
                for c0 in range(8):
                    s_gath.inc(g.dma_start(
                        out=sgin[0:16, :, c0 * 16:(c0 + 1) * 16],
                        in_=idv[16 * c0:16 * (c0 + 1), :].rearrange("q (e m) -> q e m", e=E)), 16)
            G(_)
            sgin_done = s_gath.n
            p2.close()


        if phases >= 3:
            p3 = ExitStack()
            e3 = p3.enter_context
            xg = [e3(nc.sbuf_tensor(f"xgt{i}", [128, 8, CAP], BF16)) for i in range(2)]
            pg = [e3(nc.sbuf_tensor(f"pgt{i}", [128, NBLK, 64], F32)) for i in range(2)]
            hball = e3(nc.sbuf_tensor("hball", [128, 32, CAP], BF16))
            ysb = e3(nc.sbuf_tensor("ysb", [128, NBLK, C], F32))
            h_ps = [e3(nc.psum_tensor(f"hps{i}", [128, CAP], F32)) for i in range(2)]
            y2_ps = [e3(nc.psum_tensor(f"y2ps{i}", [128, C], F32)) for i in range(2)]

            wq_snap = {}
            sg_done = [0] * E
            gd_gather = [0] * E
            gd_scatter = [0] * E
            pe_h_all = {}
            pe_y2_all = {}
            v_yscale = [0] * E

            spe0, ssc0, sv0 = spe.n, ssc.n, sv.n

            def pe_h_p(e_, hk):
                return spe0 + 37 * e_ + hk + 1

            def pe_w1done_p(e_):
                return spe0 + 37 * e_ + 32

            def pe_y2done_p(e_):
                return spe0 + 37 * e_ + 37

            def sc_gelu_p(e_, hk):
                return ssc0 + 32 * e_ + hk + 1

            def sc_geludone_p(e_):
                return ssc0 + 32 * e_ + 32

            def v_ic_p(e_, ic):
                return sv0 + 5 * e_ + ic + 1

            def v_yscale_p(e_):
                return sv0 + 5 * e_ + 5

            def wload_w1(g, e_, q):
                gq = 4 * e_ + q - 2
                if gq >= 0:
                    g.wait_ge(spe.h, pe_h_p(gq // 4, 8 * (gq % 4) + 7))
                s_w1[q % 2].inc(g.dma_start(out=w1s[q % 2][:], in_=d_w1[e_, q]), 16)
                wq_snap[(e_, q)] = (s_w1[q % 2], s_w1[q % 2].n)

            def wload_w2(g, e_, q):
                if e_ > 0:
                    g.wait_ge(spe.h, pe_y2done_p(e_ - 1))
                s_w2.inc(g.dma_start(out=w2s[:, q * 8:(q + 1) * 8, :], in_=d_w2[e_, q]), 16)
                wq_snap[(e_, 4 + q)] = (s_w2, s_w2.n)

            for e_ in range(E):
                def _(g, e_=e_):
                    if e_ == 0:
                        g.wait_ge(sv.h, v_init)
                    wload_w1(g, e_, 0)
                    wload_w1(g, e_, 1)
                    if e_ == 0:
                        g.wait_ge(sv.h, v_sel)
                        g.wait_ge(s_gath.h, sgin_done)
                    g.load_library(library_config.sparse_gather)
                    sgp.inc(g.sparse_gather(sgout[0:16, e_, :], sgin[0:16, e_, :],
                                            num_found=nfb[0:1, e_:e_ + 1]))
                    sg_done[e_] = sgp.n
                    g.wait_ge(sgp.h, sg_done[e_])
                    s_gath.inc(g.dma_start(out=idxall[0:16, e_, :],
                                           in_=sgout[0:16, e_, 0:CAP // 16]), 16)
                    s_gath.inc(g.dma_start(out=idxall[16:32, e_, :],
                                           in_=sgout[0:16, e_, 0:CAP // 16]), 16)
                    g.wait_ge(s_gath.h, s_gath.n)
                    g.load_library(library_config.mlp)
                    if e_ == 0:
                        for _sem, _val in xn_wr_snap:
                            g.wait_ge(_sem.h, _val)
                    if e_ >= 2:
                        g.wait_ge(spe.h, pe_w1done_p(e_ - 2))
                        g.wait_ge(sv.h, v_yscale_p(e_ - 2))
                    s_gath.inc(g.dma_gather(xg[e_ % 2][:], d_xn, idxall[:, e_, :], CAP, CAP, C,
                                            elem_step=C, transpose=True), 16)
                    s_gath.inc(g.dma_gather(pg[e_ % 2][:], d_pr, idxall[:, e_, :], CAP, CAP, 64,
                                            elem_step=64), 16)
                    gd_gather[e_] = s_gath.n
                    if e_ > 0:
                        g.wait_ge(sv.h, v_yscale_p(e_ - 1))
                        g.wait_ge(s_acc0.h, acc0_done)
                        if e_ > 1:
                            g.wait_ge(s_scat.h, gd_scatter[e_ - 2])
                        s_scat.inc(g.dma_scatter_add(d_acc, ysb[:], idxall[:, e_ - 1, :],
                                                     CAP, CAP, C, elem_step=C), 16)
                        gd_scatter[e_ - 1] = s_scat.n
                    wload_w1(g, e_, 2)
                    wload_w2(g, e_, 0)
                    wload_w2(g, e_, 1)
                    wload_w1(g, e_, 3)
                    wload_w2(g, e_, 2)
                    wload_w2(g, e_, 3)
                G(_)

                def _(t, e_=e_):
                    for hk in range(32):
                        q, hq = hk // 8, hk % 8
                        if hk == 0:
                            t.wait_ge(s_gath.h, gd_gather[e_])
                            if e_ == 0:
                                t.wait_ge(sv.h, v_sel)
                        _ws, _wv = wq_snap[(e_, q)]
                        t.wait_ge(_ws.h, _wv)
                        if hk >= 2:
                            t.wait_ge(ssc.h, sc_gelu_p(e_, hk - 2))
                        elif e_ > 0:
                            t.wait_ge(ssc.h, sc_geludone_p(e_ - 1))
                        ps = h_ps[hk % 2]
                        for cc in range(8):
                            t.matmul(ps[:, 0:512], w1s[q % 2][:, cc, hq * 128:(hq + 1) * 128],
                                     xg[e_ % 2][:, cc, 0:512], start=(cc == 0), stop=(cc == 7))
                            mm = t.matmul(ps[:, 512:CAP], w1s[q % 2][:, cc, hq * 128:(hq + 1) * 128],
                                          xg[e_ % 2][:, cc, 512:CAP], start=(cc == 0), stop=(cc == 7))
                        assert spe.inc(mm) == pe_h_p(e_, hk)
                        pe_h_all[(e_, hk)] = spe.n
                    for ic in range(NBLK):
                        if ic == 0:
                            t.wait_ge(ssc.h, sc_geludone_p(e_))
                            _ws, _wv = wq_snap[(e_, 4 + 3)]
                            t.wait_ge(_ws.h, _wv)
                            if e_ > 0:
                                t.wait_ge(sv.h, v_yscale_p(e_ - 1))
                        if ic >= 2:
                            t.wait_ge(sv.h, v_ic_p(e_, ic - 2))
                        ps = y2_ps[ic % 2]
                        for hk in range(32):
                            t.matmul(ps[:, 0:512], hball[:, hk, ic * 128:(ic + 1) * 128],
                                     w2s[:, hk, 0:512], start=(hk == 0), stop=(hk == 31))
                            mm = t.matmul(ps[:, 512:1024], hball[:, hk, ic * 128:(ic + 1) * 128],
                                          w2s[:, hk, 512:1024], start=(hk == 0), stop=(hk == 31))
                        assert spe.inc(mm) == spe0 + 37 * e_ + 32 + ic + 1
                        pe_y2_all[(e_, ic)] = spe.n
                PE(_)

                def _(v, e_=e_):
                    for ic in range(NBLK):
                        v.wait_ge(spe.h, pe_y2_all[(e_, ic)])
                        if ic == 0:
                            v.wait_ge(s_gath.h, gd_gather[e_])
                            if e_ > 0:
                                v.wait_ge(s_scat.h, gd_scatter[e_ - 1])
                        _n = sv.inc(v.tensor_scalar(ysb[:, ic, :], y2_ps[ic % 2][:],
                                                    pg[e_ % 2][:, ic, e_:e_ + 1], None, ALU.mult))
                        assert _n == v_ic_p(e_, ic)
                V(_)
                v_yscale[e_] = sv.n
                assert v_yscale[e_] == v_yscale_p(e_)

            def _(s):
                for e_ in range(E):
                    for hk in range(32):
                        s.wait_ge(spe.h, pe_h_all[(e_, hk)])
                        if e_ > 0 and hk == 0:
                            s.wait_ge(spe.h, pe_y2done_p(e_ - 1))
                        _n = ssc.inc(s.activation(hball[:, hk, :], h_ps[hk % 2][:], AF.Gelu))
                        assert _n == sc_gelu_p(e_, hk)
            SC(_)

            def _(g):
                g.wait_ge(sv.h, v_yscale[E - 1])
                g.wait_ge(s_scat.h, gd_scatter[E - 2])
                s_scat.inc(g.dma_scatter_add(d_acc, ysb[:], idxall[:, E - 1, :],
                                             CAP, CAP, C, elem_step=C), 16)
                gd_scatter[E - 1] = s_scat.n
            G(_)
            all_scattered = s_scat.n
            p3.close()


        if phases >= 4:
            p4 = ExitStack()
            e4 = p4.enter_context
            TDEP = 4
            acct = [e4(nc.sbuf_tensor(f"acct{i}", [128, C], F32)) for i in range(TDEP)]
            gt = [e4(nc.sbuf_tensor(f"gtt{i}", [128, C], F32)) for i in range(TDEP)]
            stl = [e4(nc.sbuf_tensor(f"stl{i}", [128, C], F32)) for i in range(TDEP)]
            xt = [e4(nc.sbuf_tensor(f"xtt{i}", [128, C], F32)) for i in range(TDEP)]
            ot = [e4(nc.sbuf_tensor(f"ott{i}", [128, C], F32)) for i in range(TDEP)]

            tail_ta = [0] * 16
            tail_tg = [0] * 16
            tail_sc = [0] * 16
            tail_v = [0] * 16
            tail_st = [0] * 16
            for k in range(16):
                par = k % TDEP
                r0 = k * 128

                def _(sync, k=k, par=par, r0=r0):
                    if k == 0:
                        sync.wait_ge(s_scat.h, all_scattered)
                    if k >= TDEP:
                        sync.wait_ge(ssc.h, tail_sc[k - TDEP])
                    s_ta[par].inc(sync.dma_start(out=acct[par][:], in_=d_acc[r0:r0 + 128, :]), 16)
                    if k % 2 == 0:
                        if k >= TDEP:
                            sync.wait_ge(sv.h, tail_v[k - TDEP])
                        s_ta[par].inc(sync.dma_start(
                            out=xt[par][:], in_=d_z[HALF + r0:HALF + r0 + 128, :]), 16)
                    tail_ta[k] = s_ta[par].n
                SY(_)

                def _(g, k=k, par=par, r0=r0):
                    if k == 0:
                        g.wait_ge(s_scat.h, all_scattered)
                    if k >= TDEP:
                        g.wait_ge(sv.h, tail_v[k - TDEP])
                    s_tg[par].inc(g.dma_start(out=stl[par][:], in_=d_s[r0:r0 + 128, :]), 16)
                    if k % 2 == 1:
                        s_tg[par].inc(g.dma_start(
                            out=xt[par][:], in_=d_z[HALF + r0:HALF + r0 + 128, :]), 16)
                    tail_tg[k] = s_tg[par].n
                G(_)

                def _(s, k=k, par=par):
                    if k == 0:
                        s.wait_ge(s_scat.h, all_scattered)
                    s.wait_ge(s_ta[par].h, tail_ta[k])
                    s.wait_ge(s_tg[par].h, tail_tg[k])
                    if k >= TDEP:
                        s.wait_ge(sv.h, tail_v[k - TDEP])
                    ssc.inc(s.activation(gt[par][:], acct[par][:], AF.Silu))
                    if k >= 1:
                        s.wait_ge(sv.h, tail_v[k - 1])
                        s_st[(k - 1) % TDEP].inc(s.dma_start(
                            out=d_out[(k - 1) * 128:k * 128, :], in_=ot[(k - 1) % TDEP][:]), 16)
                        tail_st[k - 1] = s_st[(k - 1) % TDEP].n
                SC(_)
                tail_sc[k] = ssc.n

                def _(v, k=k, par=par):
                    if k == 0:
                        v.wait_ge(s_scat.h, all_scattered)
                    v.wait_ge(ssc.h, tail_sc[k])
                    v.wait_ge(s_ta[par].h, tail_ta[k])
                    v.wait_ge(s_tg[par].h, tail_tg[k])
                    if k >= TDEP:
                        v.wait_ge(s_st[par].h, tail_st[k - TDEP])
                    sv.inc(v.tensor_tensor(ot[par][:], gt[par][:], stl[par][:], ALU.mult))
                    v.wait_ge(sv.h, sv.n)
                    sv.inc(v.tensor_tensor(ot[par][:], ot[par][:], xt[par][:], ALU.add))
                V(_)
                tail_v[k] = sv.n

            def _(s, k=15):
                s.wait_ge(sv.h, tail_v[k])
                s_st[k % TDEP].inc(s.dma_start(
                    out=d_out[k * 128:(k + 1) * 128, :], in_=ot[k % TDEP][:]), 16)
                tail_st[k] = s_st[k % TDEP].n
            SC(_)

            def _(sync):
                for _s in s_st:
                    sync.wait_ge(_s.h, _s.n)
                for _s in s_tg:
                    sync.wait_ge(_s.h, _s.n)
            SY(_)
            p4.close()



        if phases == 1:
            def _(sync):
                for sem in _allsems:
                    sync.wait_ge(sem.h, sem.n)
                if dbg:
                    s_st[0].inc(sync.dma_start(out=d_dbgprobs, in_=probs[:]), 16)
                    s_st[0].inc(sync.dma_start(out=d_dbgu, in_=Ub[:]), 16)
                    s_st[0].inc(sync.dma_start(out=d_dbga, in_=apow[:]), 16)
                sync.wait_ge(s_st[0].h, s_st[0].n)
            SY(_)
        elif phases == 2:
            def _(g):
                g.wait_ge(sv.h, v_sel)
                g.wait_ge(s_gath.h, sgin_done)
                g.load_library(library_config.sparse_gather)
                for e_ in range(E):
                    sgp.inc(g.sparse_gather(sgout[0:16, e_, :], sgin[0:16, e_, :],
                                            num_found=nfb[0:1, e_:e_ + 1]))
                g.wait_ge(sgp.h, sgp.n)
                s_gath.inc(g.dma_start(out=idxall[0:16, :, :],
                                       in_=sgout[0:16, :, 0:CAP // 16]), 16)
                g.wait_ge(s_gath.h, s_gath.n)
                s_gath.inc(g.dma_start(out=idxall[16:32, :, :], in_=idxall[0:16, :, :]), 16)
                g.wait_ge(s_gath.h, s_gath.n)
            G(_)
            sel_all = s_gath.n

            def _(v):
                v.wait_ge(s_gath.h, sel_all)
                v.wait_ge(sgp.h, sgp.n)
                sv.inc(v.tensor_copy(bdbg[:, 0:E], probs[:, 0:E]))
            V(_)

            def _(sync):
                sync.wait_ge(s_gath.h, sel_all)
                sync.wait_ge(sgp.h, sgp.n)
                sync.wait_ge(sv.h, sv.n)
                if dbg:
                    s_st[0].inc(sync.dma_start(out=d_dbgprobs, in_=probs[:]), 16)
                    s_st[0].inc(sync.dma_start(out=d_dbgtau, in_=taub[:]), 16)
                    s_st[0].inc(sync.dma_start(out=d_dbgidx, in_=idxall[:]), 16)
                    s_st[0].inc(sync.dma_start(out=d_dbgnf, in_=nfb[:]), 16)
                    s_st[0].inc(sync.dma_start(out=d_dbgbi, in_=bdbg[:]), 16)
                sync.wait_ge(s_st[0].h, s_st[0].n)
            SY(_)
        elif phases == 3:
            def _(sync):
                sync.wait_ge(s_scat.h, all_scattered)
                if dbg:
                    s_st[0].inc(sync.dma_start(out=d_dbgprobs, in_=probs[:]), 16)
                    s_st[0].inc(sync.dma_start(out=d_dbgtau, in_=taub[:]), 16)
                    s_st[0].inc(sync.dma_start(out=d_dbgidx, in_=idxall[:]), 16)
                    s_st[0].inc(sync.dma_start(out=d_dbgnf, in_=nfb[:]), 16)
                sync.wait_ge(s_st[0].h, s_st[0].n)
            SY(_)

    mybir.codegen_inst_isa_subclasses(nc)
    return nc


def _host_consts():
    j = np.arange(128)[:, None].astype(np.float32)
    t = np.arange(128)[None, :].astype(np.float32)
    expn = np.where(t >= j, t - j, np.float32(1e30)).astype(np.float32)
    maskc = (t >= j).astype(np.float32)
    tp1 = np.broadcast_to(t + 1.0, (128, 128)).astype(np.float32).copy()
    p = np.arange(128)[:, None]
    em = np.arange(128)[None, :]
    idxc = ((em % 16) * 128 + p + 1).astype(np.float32)
    return expn, maskc, tp1, idxc


_PROG_CACHE = {}


def kernel(L0_attn_Z, L0_attn_decay, L0_attn_W, L0_attn_b, L0_attn_gamma,
           L0_attn_beta, L0_ffn_gate_w, L0_ffn_w1, L0_ffn_w2):
    Z = np.ascontiguousarray(np.asarray(L0_attn_Z, dtype=np.float32))
    decay = np.asarray(L0_attn_decay, dtype=np.float32).reshape(1, 1)
    W = np.ascontiguousarray(np.asarray(L0_attn_W, dtype=np.float32))
    bat = np.ascontiguousarray(np.asarray(L0_attn_b, dtype=np.float32).reshape(1, C))
    gam = np.ascontiguousarray(np.asarray(L0_attn_gamma, dtype=np.float32).reshape(1, C))
    bet = np.ascontiguousarray(np.asarray(L0_attn_beta, dtype=np.float32).reshape(1, C))
    gw = np.ascontiguousarray(np.asarray(L0_ffn_gate_w, dtype=np.float32))
    w1 = np.asarray(L0_ffn_w1, dtype=np.float32)
    w2 = np.asarray(L0_ffn_w2, dtype=np.float32)

    w1r = np.ascontiguousarray(
        w1.reshape(E, 8, 128, 4, 1024).transpose(0, 3, 2, 1, 4))
    w2r = np.ascontiguousarray(
        w2.reshape(E, 4, 8, 128, 1024).transpose(0, 1, 3, 2, 4))
    W_t = np.ascontiguousarray(W.T)
    expn, maskc, tp1, idxc = _host_consts()

    if "nc" not in _PROG_CACHE:
        _PROG_CACHE["nc"] = build_program()
    nc = _PROG_CACHE["nc"]

    in_maps = []
    for core in range(8):
        b, h = core // 2, core % 2
        if h == 0:
            zc = np.ascontiguousarray(np.concatenate([Z[b, HALF:], Z[b, :HALF]], axis=0))
            kapv = 0.0
        else:
            zc = np.ascontiguousarray(Z[b])
            kapv = 1.0
        in_maps.append(dict(
            zc=zc,
            zct=np.ascontiguousarray(zc.T),
            kap=np.full((128, 1), kapv, np.float32),
            dec=decay,
            gw=gw, wt=W_t, batt=bat, gam=gam, bet=bet,
            w1r=w1r, w2r=w2r,
            expn=expn, maskc=maskc, tp1=tp1, idxc=idxc,
        ))

    res = run_bass_kernel_spmd(nc, in_maps, core_ids=list(range(8)))
    out = np.empty((B, T, C), np.float32)
    for core in range(8):
        b, h = core // 2, core % 2
        shard = np.asarray(res.results[core]["out"], dtype=np.float32)
        out[b, h * HALF:(h + 1) * HALF] = shard
    return out


# version 07
# speedup vs baseline: 1.0583x; 1.0583x over previous
import numpy as np
import ml_dtypes
from contextlib import ExitStack

import concourse.bass as bass
import concourse.mybir as mybir
from concourse.bass_utils import run_bass_kernel_spmd
from concourse import library_config

F32 = mybir.dt.float32
BF16 = mybir.dt.bfloat16
I16 = mybir.dt.int16
U32 = mybir.dt.uint32
AF = mybir.ActivationFunctionType
ALU = mybir.AluOpType
AX = mybir.AxisListType

B, T, C, HDIM, E = 4, 4096, 1024, 4096, 8
HALF = T // 2
NCH = T // 128
OWN0 = NCH // 2
CAP = 640
DUMMY = HALF
KSEL = 1024
EPS = 1e-5
NBLK = CAP // 128
BISECT_ITERS = 32

BF = ml_dtypes.bfloat16


class Sem:
    def __init__(self, nc, stack, name):
        self.h = stack.enter_context(nc.semaphore(name))
        self.n = 0

    def inc(self, instr, k=1):
        instr.then_inc(self.h, k)
        self.n += k
        return self.n


def build_program(phases=4, dbg=False, nch=NCH, ysub=6, serial=False, skip=()):
    nc = bass.Bass("TRN2", target_bir_lowering=False)

    d_z = nc.dram_tensor("zc", [T, C], F32, kind="ExternalInput").ap()
    d_zt = nc.dram_tensor("zct", [C, T], F32, kind="ExternalInput").ap()
    d_kap = nc.dram_tensor("kap", [128, 1], F32, kind="ExternalInput").ap()
    d_dec = nc.dram_tensor("dec", [1, 1], F32, kind="ExternalInput").ap()
    d_gw = nc.dram_tensor("gw", [C, E], F32, kind="ExternalInput").ap()
    d_wt = nc.dram_tensor("wt", [C, C], F32, kind="ExternalInput").ap()
    d_batt = nc.dram_tensor("batt", [1, C], F32, kind="ExternalInput").ap()
    d_gam = nc.dram_tensor("gam", [1, C], F32, kind="ExternalInput").ap()
    d_bet = nc.dram_tensor("bet", [1, C], F32, kind="ExternalInput").ap()
    d_w1 = nc.dram_tensor("w1r", [E, 4, 128, 8, 1024], F32, kind="ExternalInput").ap()
    d_w2 = nc.dram_tensor("w2r", [E, 4, 128, 8, 1024], F32, kind="ExternalInput").ap()
    d_expn = nc.dram_tensor("expn", [128, 128], F32, kind="ExternalInput").ap()
    d_mask = nc.dram_tensor("maskc", [128, 128], F32, kind="ExternalInput").ap()
    d_tp1 = nc.dram_tensor("tp1", [128, 128], F32, kind="ExternalInput").ap()
    d_idxc = nc.dram_tensor("idxc", [128, 128], F32, kind="ExternalInput").ap()

    d_xn = nc.dram_tensor("xn_s", [HALF + 1, C], BF16, kind="ExternalOutput" if dbg else "Internal").ap()
    d_pr = nc.dram_tensor("pr_s", [HALF + 1, 64], F32, kind="ExternalOutput" if dbg else "Internal").ap()
    d_s = nc.dram_tensor("s_s", [HALF, C], F32, kind="ExternalOutput" if dbg else "Internal").ap()
    d_acc = nc.dram_tensor("acc_s", [HALF + 1, C], F32, kind="ExternalOutput" if dbg else "Internal").ap()
    d_out = nc.dram_tensor("out", [HALF, C], F32, kind="ExternalOutput").ap()
    if dbg:
        d_dbgprobs = nc.dram_tensor("dbgprobs", [128, NCH * E], F32, kind="ExternalOutput").ap()
        d_dbgtau = nc.dram_tensor("dbgtau", [128, E], F32, kind="ExternalOutput").ap()
        d_dbgidx = nc.dram_tensor("dbgidx", [128, E, CAP // 16], I16, kind="ExternalOutput").ap()
        d_dbgnf = nc.dram_tensor("dbgnf", [1, E], U32, kind="ExternalOutput").ap()
        d_dbgu = nc.dram_tensor("dbgu", [128, 128], BF16, kind="ExternalOutput").ap()
        d_dbga = nc.dram_tensor("dbga", [128, 128], F32, kind="ExternalOutput").ap()
        d_dbgbi = nc.dram_tensor("dbgbi", [128, 4 * E], F32, kind="ExternalOutput").ap()

    with ExitStack() as ctx:
        en = ctx.enter_context

        probs = en(nc.sbuf_tensor("probs", [128, NCH * E], F32))
        Ub = en(nc.sbuf_tensor("Ub", [128, 128], BF16))
        apow = en(nc.sbuf_tensor("apow", [128, 128], F32))
        kap = en(nc.sbuf_tensor("kapt", [128, 1], F32))
        zcol = en(nc.sbuf_tensor("zcol", [128, 1], F32))
        ones128b = en(nc.sbuf_tensor("ones128b", [128, 1], BF16))
        zxnb = en(nc.sbuf_tensor("zxnb", [1, C], BF16))
        onesm = en(nc.sbuf_tensor("onesm", [128, 128], F32))
        sgin = en(nc.sbuf_tensor("sgin", [16, E, 168], I16))
        sgout = en(nc.sbuf_tensor("sgout", [16, E, 168], I16))
        idxall = en(nc.sbuf_tensor("idxall", [128, E, CAP // 16], I16))
        nfb = en(nc.sbuf_tensor("nfb", [1, E], U32))
        taub = en(nc.sbuf_tensor("taub", [128, E], F32))
        bdbg = en(nc.sbuf_tensor("bdbg", [128, 4 * E], F32))
        idxc = en(nc.sbuf_tensor("idxcs", [128, 128], F32))
        w1s = [en(nc.sbuf_tensor(f"w1s{i}", [128, 8, 1024], BF16)) for i in range(2)]
        w2s = en(nc.sbuf_tensor("w2s", [128, 32, 1024], BF16))

        s_in = Sem(nc, ctx, "s_in")
        s_gw = Sem(nc, ctx, "s_gw")
        s_zc = [Sem(nc, ctx, f"s_zc{i}") for i in range(3)]
        s_zt = [Sem(nc, ctx, f"s_zt{i}") for i in range(3)]
        s_wt = Sem(nc, ctx, "s_wt")
        s_acc0 = Sem(nc, ctx, "s_acc0")
        s_sw = [Sem(nc, ctx, f"s_sw{i}") for i in range(2)]
        s_xw = [Sem(nc, ctx, f"s_xw{i}") for i in range(3)]
        s_pw = Sem(nc, ctx, "s_pw")
        s_w1 = [Sem(nc, ctx, f"s_w1{i}") for i in range(2)]
        s_w2 = Sem(nc, ctx, "s_w2")
        s_gath = Sem(nc, ctx, "s_gath")
        s_scat = Sem(nc, ctx, "s_scat")
        s_ta = [Sem(nc, ctx, f"s_ta{i}") for i in range(4)]
        s_tg = [Sem(nc, ctx, f"s_tg{i}") for i in range(4)]
        s_st = [Sem(nc, ctx, f"s_st{i}") for i in range(4)]
        sgp = Sem(nc, ctx, "sgp")
        sv = Sem(nc, ctx, "sv")
        ssc = Sem(nc, ctx, "ssc")
        spe = Sem(nc, ctx, "spe")

        block = en(nc.Block())
        _allsems = ([s_in, s_gw, s_wt, s_acc0, s_pw, s_w2, s_gath, s_scat,
                     sgp, sv, ssc, spe]
                    + s_zc + s_zt + s_sw + s_xw + s_w1 + s_ta + s_tg + s_st)

        def _wrap(emitter):
            def outer(f):
                if not serial:
                    return emitter(f)

                def g(eng):
                    for _s in _allsems:
                        if _s.n:
                            eng.wait_ge(_s.h, _s.n)
                    return f(eng)
                return emitter(g)
            return outer

        SY, V, SC, PE, G = (_wrap(block.sync), _wrap(block.vector),
                            _wrap(block.scalar), _wrap(block.tensor),
                            _wrap(block.gpsimd))

        p0 = ExitStack()
        e0 = p0.enter_context
        cexpn = e0(nc.sbuf_tensor("cexpn", [128, 128], F32))
        cmask = e0(nc.sbuf_tensor("cmask", [128, 128], F32))
        ctp1 = e0(nc.sbuf_tensor("ctp1", [128, 128], F32))
        dect = e0(nc.sbuf_tensor("dect", [128, 1], F32))
        lnab = e0(nc.sbuf_tensor("lnab", [128, 1], F32))
        uexp = e0(nc.sbuf_tensor("uexp", [128, 128], F32))

        def _(sync):
            for dst, src in [(cexpn, d_expn), (cmask, d_mask), (ctp1, d_tp1),
                             (kap, d_kap), (idxc, d_idxc)]:
                s_in.inc(sync.dma_start(out=dst[:], in_=src), 16)
            s_in.inc(sync.dma_start(out=dect[:], in_=d_dec.partition_broadcast(128)), 16)
        SY(_)
        p0_loads = s_in.n

        def _(v):
            v.memset(zcol[:], 0.0)
            v.memset(zxnb[:], 0.0)
            v.memset(ones128b[:], 1.0)
            v.memset(onesm[:], 1.0)
            v.memset(sgin[:, :, 128:168], DUMMY)
            v.memset(idxall[:], 0)
            if dbg:
                v.memset(bdbg[:], 0.0)
                v.memset(taub[:], 0.0)
                v.memset(nfb[:], 0)
            sv.inc(v.engine_nop())
        V(_)
        v_init = sv.n

        def _(s):
            s.wait_ge(s_in.h, p0_loads)
            ssc.inc(s.activation(lnab[:], dect[:], AF.Sigmoid))
            s.wait_ge(ssc.h, ssc.n)
            ssc.inc(s.activation(lnab[:], lnab[:], AF.Ln))
        SC(_)
        sc_lna = ssc.n

        def _(v):
            v.wait_ge(ssc.h, sc_lna)
            sv.inc(v.tensor_scalar(uexp[:], cexpn[:], lnab[:, 0:1], None, ALU.mult))
        V(_)
        v_uexp = sv.n

        def _(s):
            s.wait_ge(sv.h, v_uexp)
            ssc.inc(s.activation(uexp[:], uexp[:], AF.Exp))
        SC(_)
        sc_uexp = ssc.n

        def _(v):
            v.wait_ge(ssc.h, sc_uexp)
            sv.inc(v.tensor_tensor(Ub[:], uexp[:], cmask[:], ALU.mult))
            v.wait_ge(sv.h, sv.n)
            sv.inc(v.tensor_scalar(uexp[:], ctp1[:], lnab[:, 0:1], None, ALU.mult))
        V(_)
        v_ap1 = sv.n

        def _(s):
            s.wait_ge(sv.h, v_ap1)
            ssc.inc(s.activation(apow[:], uexp[:], AF.Exp))
        SC(_)
        sc_apow = ssc.n
        p0.close()

        p1 = ExitStack()
        e1 = p1.enter_context
        zero = e1(nc.sbuf_tensor("zerot", [128, C], F32))
        ZBUF = 3
        zc = [e1(nc.sbuf_tensor(f"zct{i}", [128, C], F32)) for i in range(ZBUF)]
        ztc = [e1(nc.sbuf_tensor(f"ztct{i}", [128, 8, 128], F32)) for i in range(3)]
        sq = e1(nc.sbuf_tensor("sq", [128, C], F32))
        xnb = [e1(nc.sbuf_tensor(f"xnb{i}", [128, C], BF16)) for i in range(3)]
        sfm = [e1(nc.sbuf_tensor(f"sfm{i}", [128, 8, 128], BF16)) for i in range(3)]
        carrk = e1(nc.sbuf_tensor("carrk", [128, 8], F32))
        yb = [e1(nc.sbuf_tensor(f"yb{i}", [128, C], F32)) for i in range(2)]
        ynb = [e1(nc.sbuf_tensor(f"ynbt{i}", [128, C], BF16)) for i in range(2)]
        stile = [e1(nc.sbuf_tensor(f"stile{i}", [128, C], F32)) for i in range(2)]
        wtb = e1(nc.sbuf_tensor("wtb", [128, 8, C], BF16))
        gws = e1(nc.sbuf_tensor("gws", [128, 8, E], F32))
        csb = e1(nc.sbuf_tensor("csb", [128, E], F32))
        cspart = e1(nc.sbuf_tensor("cspart", [128, E], F32))
        batb = e1(nc.sbuf_tensor("batb", [128, C], F32))
        gamb = e1(nc.sbuf_tensor("gamb", [128, C], BF16))
        betb = e1(nc.sbuf_tensor("betb", [128, C], BF16))

        def st2(name):
            return e1(nc.sbuf_tensor(name, [128, 2], F32))

        zsum, ssq, meant, vart, rstd, nrstd, nbias = (st2(n) for n in
            ("zsum", "ssq", "meant", "vart", "rstdt", "nrstdt", "nbiast"))
        rmax, rsum, rcp, t128a = (st2(n) for n in ("rmaxt", "rsumt", "rcpt", "t128a"))
        ysum, yssq, ymean, yvar, yrstd, ynbias, yt = (st2(n) for n in
            ("ysumt", "yssqt", "ymeant", "yvart", "yrstdt", "ynbiast", "ytt"))
        scoret = e1(nc.sbuf_tensor("scoret", [128, 2, E], F32))
        expt = e1(nc.sbuf_tensor("exptt", [128, 2, E], F32))

        gate_ps = [e1(nc.psum_tensor(f"gateps{i}", [128, E], F32)) for i in range(2)]
        scan_ps = [e1(nc.psum_tensor(f"scanps{i}", [128, 8, 128], F32)) for i in range(2)]
        y_ps = e1(nc.psum_tensor("yps", [128, C], F32))
        b_ps = y_ps

        def _(v):
            v.wait_ge(ssc.h, sc_apow)
            sv.inc(v.memset(zero[:], 0.0))
        V(_)
        v_zero = sv.n

        def _(sync):
            sync.wait_ge(sv.h, v_zero)
            sync.wait_ge(sv.h, v_init)
            for i in range(17):
                rows = min(128, HALF + 1 - i * 128)
                s_acc0.inc(sync.dma_start(out=d_acc[i * 128:i * 128 + rows, :],
                                          in_=zero[0:rows, :]), 16)
            s_acc0.inc(sync.dma_start(out=d_xn[DUMMY:DUMMY + 1, :], in_=zxnb[:]), 16)
            s_acc0.inc(sync.dma_start(out=d_pr[DUMMY:DUMMY + 1, :], in_=zero[0:1, 0:64]), 16)
            for i in range(16):
                s_acc0.inc(sync.dma_start(out=d_pr[i * 128:(i + 1) * 128, E:64],
                                          in_=zero[0:128, 0:64 - E]), 16)
        SY(_)
        acc0_done = s_acc0.n

        def _(sync):
            sync.wait_ge(ssc.h, sc_apow)
            s_gw.inc(sync.dma_start(out=gws[:], in_=d_gw.rearrange("(cc p) e -> p cc e", p=128)), 16)
            s_gw.inc(sync.dma_start(out=batb[:], in_=d_batt.partition_broadcast(128)), 16)
        SY(_)
        gw_loaded = s_gw.n

        def _(g):
            g.wait_ge(sv.h, v_init)
            g.wait_ge(ssc.h, sc_apow)
            s_wt.inc(g.dma_start(out=wtb[:], in_=d_wt.rearrange("(dc p) e -> p dc e", p=128)), 16)
            s_wt.inc(g.dma_start(out=gamb[:], in_=d_gam.partition_broadcast(128)), 16)
            s_wt.inc(g.dma_start(out=betb[:], in_=d_bet.partition_broadcast(128)), 16)
        G(_)
        wtb_loaded = s_wt.n

        def _(v):
            v.wait_ge(s_gw.h, gw_loaded)
            v.wait_ge(sv.h, v_init)
            sv.inc(v.reduce_sum(cspart[:], gws[:].rearrange("p cc e -> p e cc"), axis=AX.X))
        V(_)
        v_cspart = sv.n

        def _(t):
            t.wait_ge(sv.h, v_cspart)
            spe.inc(t.matmul(b_ps[:, 0:E], onesm[:], cspart[:], start=True, stop=True))
        PE(_)
        pe_csb = spe.n

        def _(v):
            v.wait_ge(spe.h, pe_csb)
            sv.inc(v.tensor_copy(csb[:], b_ps[:, 0:E]))
        V(_)
        v_bcast = sv.n

        zc_load = [0] * NCH
        ztc_load = [0] * NCH
        v_var = [0] * NCH
        v_stats = [0] * NCH
        v_sm1 = [0] * NCH
        v_probs = [0] * NCH
        v_scan = [0] * NCH
        v_yevac = [0] * NCH
        v_yvar = [0] * NCH
        v_ynb = [0] * NCH
        v_s = [0] * NCH
        pe_gate = [0] * NCH
        pe_scan = [0] * NCH
        pe_y = [0] * NCH
        dma_sxnp = [0] * NCH
        dma_xwp = [0] * NCH

        sscp_ss = [0] * NCH
        sscp_rstd = [0] * NCH
        sscp_cast = [0] * NCH
        sscp_exp = [0] * NCH
        sscp_yss = [0] * NCH
        sscp_yrstd = [0] * NCH
        sscp_yn = [0] * NCH
        _c = ssc.n
        for m in range(nch):
            if m - 2 >= OWN0:
                _c += 1; sscp_yss[m - 2] = _c
                _c += 2; sscp_yrstd[m - 2] = _c
                _c += 1; sscp_yn[m - 2] = _c
            _c += 1; sscp_ss[m] = _c
            _c += 2; sscp_rstd[m] = _c
            _c += 1; sscp_cast[m] = _c
            _c += 1; sscp_exp[m] = _c
        for mm_ in range(max(OWN0, nch - 2), nch):
            _c += 1; sscp_yss[mm_] = _c
            _c += 2; sscp_yrstd[mm_] = _c
            _c += 1; sscp_yn[mm_] = _c

        def emit_yside(s, mm, barrier):
            pam = mm % 2
            ybp, ynp = yb[pam], ynb[pam]
            s.wait_ge(sv.h, v_yevac[mm])
            s.wait_ge(ssc.h, barrier)
            _n = ssc.inc(s.activation(sq[:], ybp[:], AF.Square,
                                      accum_out=yssq[:, pam:pam + 1]))
            assert _n == sscp_yss[mm], (mm, _n)
            s.wait_ge(sv.h, v_yvar[mm])
            ssc.inc(s.activation(yrstd[:, pam:pam + 1], yvar[:, pam:pam + 1], AF.Ln))
            s.wait_ge(ssc.h, ssc.n)
            _n = ssc.inc(s.activation(yrstd[:, pam:pam + 1], yrstd[:, pam:pam + 1],
                                      AF.Exp, scale=-0.5))
            assert _n == sscp_yrstd[mm], (mm, _n)
            s.wait_ge(sv.h, v_ynb[mm])
            if mm - 2 >= OWN0:
                s.wait_ge(sv.h, v_s[mm - 2])
            _n = ssc.inc(s.activation(ynp[:], ybp[:], AF.Identity,
                                      bias=ynbias[:, pam:pam + 1],
                                      scale=yrstd[:, pam:pam + 1]))
            assert _n == sscp_yn[mm], (mm, _n)

        def emit_yvec(my):
            pam = my % 2
            yb_t, yn_t, st_t = yb[pam], ynb[pam], stile[pam]
            xby = xnb[my % 3]

            def _(v, my=my, yb_t=yb_t, pam=pam):
                v.wait_ge(spe.h, pe_y[my])
                if my >= OWN0 + 2:
                    v.wait_ge(ssc.h, sscp_yn[my - 2])
                sv.inc(v.tensor_tensor(yb_t[:], y_ps[:], batb[:], ALU.add))
                v.wait_ge(sv.h, sv.n)
                sv.inc(v.reduce_sum(ysum[:, pam:pam + 1], yb_t[:], axis=AX.X))
                v.wait_ge(sv.h, sv.n)
                sv.inc(v.tensor_scalar(ymean[:, pam:pam + 1], ysum[:, pam:pam + 1],
                                       1.0 / C, None, ALU.mult))
            V(_)
            v_yevac[my] = sv.n

            def _(v, my=my, pam=pam):
                v.wait_ge(ssc.h, sscp_yss[my])
                v.tensor_scalar(yvar[:, pam:pam + 1], yssq[:, pam:pam + 1],
                                1.0 / C, EPS, ALU.mult, ALU.add)
                sv.inc(v.tensor_tensor(yt[:, pam:pam + 1], ymean[:, pam:pam + 1],
                                       ymean[:, pam:pam + 1], ALU.mult))
                v.wait_ge(sv.h, sv.n)
                sv.inc(v.tensor_sub(yvar[:, pam:pam + 1], yvar[:, pam:pam + 1],
                                    yt[:, pam:pam + 1]))
            V(_)
            v_yvar[my] = sv.n

            def _(v, my=my, pam=pam):
                v.wait_ge(ssc.h, sscp_yrstd[my])
                sv.inc(v.tensor_scalar(yt[:, pam:pam + 1], yrstd[:, pam:pam + 1],
                                       -1.0, None, ALU.mult))
                v.wait_ge(sv.h, sv.n)
                sv.inc(v.tensor_tensor(ynbias[:, pam:pam + 1], ymean[:, pam:pam + 1],
                                       yt[:, pam:pam + 1], ALU.mult))
            V(_)
            v_ynb[my] = sv.n

            def _(v, my=my, yn_t=yn_t, st_t=st_t, pam=pam):
                v.wait_ge(ssc.h, sscp_yn[my])
                v.wait_ge(s_wt.h, wtb_loaded)
                if my >= OWN0 + 2:
                    v.wait_ge(s_sw[pam].h, dma_sxnp[my - 2])
                sv.inc(v.tensor_tensor(st_t[:], yn_t[:], gamb[:], ALU.mult))
                v.wait_ge(sv.h, sv.n)
                sv.inc(v.tensor_tensor(st_t[:], st_t[:], betb[:], ALU.add))
            V(_)
            v_s[my] = sv.n

            def _(g, my=my, st_t=st_t, xby=xby, pam=pam):
                r0 = (my - OWN0) * 128
                g.wait_ge(sv.h, v_s[my])
                s_sw[pam].inc(g.dma_start(out=d_s[r0:r0 + 128, :], in_=st_t[:]), 16)
                dma_sxnp[my] = s_sw[pam].n
                g.wait_ge(ssc.h, sscp_cast[my])
                s_xw[my % 3].inc(g.dma_start(out=d_xn[r0:r0 + 128, :], in_=xby[:]), 16)
                dma_xwp[my] = s_xw[my % 3].n
                g.wait_ge(sv.h, v_probs[my])
                s_pw.inc(g.dma_start(out=d_pr[r0:r0 + 128, 0:E],
                                     in_=probs[:, my * E:(my + 1) * E]), 16)
            G(_)

        def _(sync):
            for mm_ in range(min(2, nch)):
                sync.wait_ge(ssc.h, sc_apow)
                s_zt[mm_ % 3].inc(sync.dma_start(
                    out=ztc[mm_ % 3][:],
                    in_=d_zt[:, mm_ * 128:(mm_ + 1) * 128].rearrange(
                        "(cc p) t -> p cc t", p=128)), 16)
                ztc_load[mm_] = s_zt[mm_ % 3].n
        SY(_)

        for m in range(nch):
            par = m % 2
            zb, tb, xb, sf = zc[m % ZBUF], ztc[m % 3], xnb[m % 3], sfm[m % 3]
            gp, sp = gate_ps[par], scan_ps[par]

            def _(sync, m=m, zb=zb):
                if m >= 3:
                    sync.wait_ge(ssc.h, sscp_cast[m - 3])
                s_zc[m % 3].inc(sync.dma_start(out=zb[:], in_=d_z[m * 128:(m + 1) * 128, :]), 16)
                zc_load[m] = s_zc[m % 3].n
            SY(_)

            def _(v, m=m, zb=zb, par=par):
                v.wait_ge(s_zc[m % 3].h, zc_load[m])
                if m >= 2:
                    v.wait_ge(ssc.h, sscp_cast[m - 2])
                v.reduce_sum(zsum[:, par:par + 1], zb[:], axis=AX.X)
                v.wait_ge(ssc.h, sscp_ss[m])
                sv.inc(v.tensor_scalar(vart[:, par:par + 1], ssq[:, par:par + 1],
                                       1.0 / C, EPS, ALU.mult, ALU.add))
                v.wait_ge(sv.h, sv.n)
                sv.inc(v.tensor_scalar(meant[:, par:par + 1], zsum[:, par:par + 1],
                                       1.0 / C, None, ALU.mult))
                v.wait_ge(sv.h, sv.n)
                sv.inc(v.tensor_tensor(t128a[:, par:par + 1], meant[:, par:par + 1],
                                       meant[:, par:par + 1], ALU.mult))
                v.wait_ge(sv.h, sv.n)
                sv.inc(v.tensor_sub(vart[:, par:par + 1], vart[:, par:par + 1],
                                    t128a[:, par:par + 1]))
            V(_)
            v_var[m] = sv.n

            def _(v, m=m, par=par):
                v.wait_ge(ssc.h, sscp_rstd[m])
                if m >= 2:
                    v.wait_ge(ssc.h, sscp_cast[m - 2])
                sv.inc(v.tensor_scalar(nrstd[:, par:par + 1], rstd[:, par:par + 1],
                                       -1.0, None, ALU.mult))
                v.wait_ge(sv.h, sv.n)
                sv.inc(v.tensor_tensor(nbias[:, par:par + 1], meant[:, par:par + 1],
                                       nrstd[:, par:par + 1], ALU.mult))
            V(_)
            v_stats[m] = sv.n

            def _(t, m=m, tb=tb, gp=gp):
                t.wait_ge(s_zt[m % 3].h, ztc_load[m])
                t.wait_ge(s_gw.h, gw_loaded)
                if m >= 2:
                    t.wait_ge(sv.h, v_sm1[m - 2])
                for cc in range(8):
                    mm = t.matmul(gp[:, 0:E], tb[:, cc, :], gws[:, cc, :],
                                  start=(cc == 0), stop=(cc == 7))
                spe.inc(mm)
            PE(_)
            pe_gate[m] = spe.n

            def _(v, m=m, gp=gp, par=par):
                v.wait_ge(spe.h, pe_gate[m])
                if m == 0:
                    v.wait_ge(sv.h, v_bcast)
                if m >= 2:
                    v.wait_ge(ssc.h, sscp_exp[m - 2])
                sv.inc(v.scalar_tensor_tensor(scoret[:, par, :], csb[:], meant[:, par:par + 1],
                                              gp[:, 0:E], ALU.mult, ALU.subtract))
                v.wait_ge(sv.h, sv.n)
                sv.inc(v.tensor_scalar(scoret[:, par, :], scoret[:, par, :],
                                       nrstd[:, par:par + 1], None, ALU.mult))
                v.wait_ge(sv.h, sv.n)
                sv.inc(v.reduce_max(rmax[:, par:par + 1], scoret[:, par, :], axis=AX.X))
                v.wait_ge(sv.h, sv.n)
                sv.inc(v.tensor_scalar(rmax[:, par:par + 1], rmax[:, par:par + 1],
                                       -1.0, None, ALU.mult))
            V(_)
            v_sm1[m] = sv.n

            def _(v, m=m, par=par):
                v.wait_ge(ssc.h, sscp_exp[m])
                sv.inc(v.reduce_sum(rsum[:, par:par + 1], expt[:, par, :], axis=AX.X))
                v.wait_ge(sv.h, sv.n)
                sv.inc(v.reciprocal(rcp[:, par:par + 1], rsum[:, par:par + 1]))
                v.wait_ge(sv.h, sv.n)
                sv.inc(v.tensor_scalar(probs[:, m * E:(m + 1) * E], expt[:, par, :],
                                       rcp[:, par:par + 1], None, ALU.mult))
            V(_)
            v_probs[m] = sv.n

            def _(t, m=m, xb=xb, sp=sp):
                t.wait_ge(ssc.h, sscp_cast[m])
                t.wait_ge(ssc.h, sc_apow)
                if m >= 2:
                    t.wait_ge(sv.h, v_scan[m - 2])
                for dc in range(8):
                    mm = t.matmul(sp[:, dc, :], xb[:, dc * 128:(dc + 1) * 128], Ub[:],
                                  start=True, stop=True)
                spe.inc(mm)
            PE(_)
            pe_scan[m] = spe.n

            def _(v, m=m, sp=sp, sf=sf):
                v.wait_ge(spe.h, pe_scan[m])
                if m >= 3 and m - 3 >= OWN0:
                    v.wait_ge(spe.h, pe_y[m - 3])
                if m == 0:
                    for dc in range(8):
                        mmv = v.tensor_copy(carrk[:, dc:dc + 1], zcol[:])
                else:
                    prev = sfm[(m - 1) % 3]
                    for dc in range(8):
                        mmv = v.tensor_copy(carrk[:, dc:dc + 1], prev[:, dc, 127:128])
                    if m == OWN0:
                        sv.inc(mmv)
                        v.wait_ge(sv.h, sv.n)
                        for dc in range(8):
                            mmv = v.tensor_tensor(carrk[:, dc:dc + 1], carrk[:, dc:dc + 1],
                                                  kap[:], ALU.mult)
                sv.inc(mmv)
                v.wait_ge(sv.h, sv.n)
                for dc in range(8):
                    last = v.scalar_tensor_tensor(sf[:, dc, :], apow[:], carrk[:, dc:dc + 1],
                                                  sp[:, dc, :], ALU.mult, ALU.add)
                sv.inc(last)
            V(_)
            v_scan[m] = sv.n

            if m - 1 >= OWN0:
                emit_yvec(m - 1)

            if m >= OWN0:
                def _(t, m=m, sf=sf):
                    t.wait_ge(sv.h, v_scan[m])
                    if m == OWN0:
                        t.wait_ge(s_wt.h, wtb_loaded)
                    if m > OWN0:
                        t.wait_ge(sv.h, v_yevac[m - 1])
                    for dc in range(8):
                        t.matmul(y_ps[:, 0:512], sf[:, dc, :], wtb[:, dc, 0:512],
                                 start=(dc == 0), stop=(dc == 7))
                        mm = t.matmul(y_ps[:, 512:1024], sf[:, dc, :], wtb[:, dc, 512:1024],
                                      start=(dc == 0), stop=(dc == 7))
                    spe.inc(mm)
                PE(_)
                pe_y[m] = spe.n

            def _(s, m=m, zb=zb, xb=xb, par=par):
                if m - 2 >= OWN0:
                    emit_yside(s, m - 2, sscp_exp[m - 1])
                s.wait_ge(s_zc[m % 3].h, zc_load[m])
                if m >= 2:
                    s.wait_ge(sv.h, v_var[m - 2])
                if m - 2 >= OWN0:
                    s.wait_ge(ssc.h, sscp_yn[m - 2])
                elif m >= 1:
                    s.wait_ge(ssc.h, sscp_exp[m - 1])
                _n = ssc.inc(s.activation(sq[:], zb[:], AF.Square,
                                          accum_out=ssq[:, par:par + 1]))
                assert _n == sscp_ss[m], (m, _n)
                s.wait_ge(sv.h, v_var[m])
                ssc.inc(s.activation(rstd[:, par:par + 1], vart[:, par:par + 1], AF.Ln))
                s.wait_ge(ssc.h, ssc.n)
                _n = ssc.inc(s.activation(rstd[:, par:par + 1], rstd[:, par:par + 1],
                                          AF.Exp, scale=-0.5))
                assert _n == sscp_rstd[m], (m, _n)
                s.wait_ge(sv.h, v_stats[m])
                if m >= 3:
                    s.wait_ge(spe.h, pe_scan[m - 3])
                    if m - 3 >= OWN0:
                        s.wait_ge(s_xw[m % 3].h, dma_xwp[m - 3])
                _n = ssc.inc(s.activation(xb[:], zb[:], AF.Identity,
                                          bias=nbias[:, par:par + 1],
                                          scale=rstd[:, par:par + 1]))
                assert _n == sscp_cast[m], (m, _n)
                s.wait_ge(sv.h, v_sm1[m])
                _n = ssc.inc(s.activation(expt[:, par, :], scoret[:, par, :], AF.Exp,
                                          bias=rmax[:, par:par + 1]))
                assert _n == sscp_exp[m], (m, _n)
                if m + 2 < nch:
                    if m >= 1:
                        s.wait_ge(spe.h, pe_gate[m - 1])
                    s_zt[(m + 2) % 3].inc(s.dma_start(
                        out=ztc[(m + 2) % 3][:],
                        in_=d_zt[:, (m + 2) * 128:(m + 3) * 128].rearrange(
                            "(cc p) t -> p cc t", p=128)), 16)
                    ztc_load[m + 2] = s_zt[(m + 2) % 3].n
            SC(_)

        if nch - 1 >= OWN0:
            emit_yvec(nch - 1)

            def _(s):
                for mm_ in range(max(OWN0, nch - 2), nch):
                    emit_yside(s, mm_, ssc.n)
            SC(_)
        assert ssc.n == _c, (ssc.n, _c)

        p1_v_done = sv.n
        p1_sc_done = ssc.n
        p1_pe_done = spe.n
        xn_wr_snap = [(_sem, _sem.n) for _sem in s_xw + s_sw + [s_pw]]
        p1.close()

        if phases >= 2:
            p2 = ExitStack()
            e2 = p2.enter_context
            lo = e2(nc.sbuf_tensor("lot", [128, E], F32))
            hi = e2(nc.sbuf_tensor("hit", [128, E], F32))
            mid = e2(nc.sbuf_tensor("midt", [128, E], F32))
            dec2 = e2(nc.sbuf_tensor("dec2", [128, E], F32))
            td1 = e2(nc.sbuf_tensor("td1", [128, E], F32))
            td2 = e2(nc.sbuf_tensor("td2", [128, E], F32))
            ind = e2(nc.sbuf_tensor("indt", [128, NCH, E], BF16))
            partb = e2(nc.sbuf_tensor("partb", [128, E], F32))
            msk = e2(nc.sbuf_tensor("mskt", [128, 128], F32))
            idv = e2(nc.sbuf_tensor("idvt", [128, 128], F32))
            cnt_ps = e2(nc.psum_tensor("cntps", [128, E], F32))

            def _(v):
                v.wait_ge(sv.h, p1_v_done)
                v.wait_ge(ssc.h, p1_sc_done)
                for _sem, _val in xn_wr_snap:
                    v.wait_ge(_sem.h, _val)
                v.wait_ge(spe.h, p1_pe_done)
                for _sem in [s_in, s_gw, s_wt, s_acc0] + s_zc + s_zt:
                    v.wait_ge(_sem.h, _sem.n)
                if dbg:
                    sv.inc(v.tensor_copy(bdbg[:, 2 * E:3 * E], probs[:, 0:E]))
                v.memset(lo[:], 0.0)
                v.memset(hi[:], 1.1)
                sv.inc(v.memset(mid[:], 0.55))
            V(_)
            v_tb = sv.n

            for it in range(BISECT_ITERS):
                def _(v, v_tb=v_tb):
                    v.wait_ge(sv.h, v_tb)
                    sv.inc(v.tensor_tensor(
                        ind[:],
                        probs[:].rearrange("p (m e) -> p m e", e=E),
                        mid[:].unsqueeze(1).to_broadcast([128, NCH, E]),
                        ALU.is_ge))
                    v.wait_ge(sv.h, sv.n)
                    sv.inc(v.reduce_sum(partb[:], ind[:].rearrange("p m e -> p e m"),
                                        axis=AX.X))
                V(_)
                v_cmp = sv.n

                def _(t, v_cmp=v_cmp):
                    t.wait_ge(sv.h, v_cmp)
                    spe.inc(t.matmul(cnt_ps[:], onesm[:], partb[:], start=True, stop=True))
                PE(_)
                pe_cnt = spe.n

                def _(v, pe_cnt=pe_cnt):
                    v.wait_ge(spe.h, pe_cnt)
                    v.tensor_sub(td1[:], mid[:], lo[:])
                    v.tensor_sub(td2[:], hi[:], mid[:])
                    sv.inc(v.tensor_scalar(dec2[:], cnt_ps[:], float(KSEL), None, ALU.is_ge))
                    v.wait_ge(sv.h, sv.n)
                    v.tensor_tensor(td1[:], td1[:], dec2[:], ALU.mult)
                    sv.inc(v.tensor_tensor(td2[:], td2[:], dec2[:], ALU.mult))
                    v.wait_ge(sv.h, sv.n)
                    v.tensor_add(lo[:], lo[:], td1[:])
                    sv.inc(v.tensor_add(hi[:], mid[:], td2[:]))
                    v.wait_ge(sv.h, sv.n)
                    sv.inc(v.tensor_add(td1[:], lo[:], hi[:]))
                    v.wait_ge(sv.h, sv.n)
                    sv.inc(v.tensor_scalar(mid[:], td1[:], 0.5, None, ALU.mult))
                V(_)
                v_tb = sv.n

            def _(v, v_tb=v_tb):
                v.wait_ge(sv.h, v_tb)
                v.tensor_copy(taub[:], lo[:])
                sv.inc(v.tensor_tensor(
                    msk[:].rearrange("p (e m) -> p e m", e=E),
                    probs[:, OWN0 * E:NCH * E].rearrange("p (m e) -> p e m", e=E),
                    lo[:].unsqueeze(2).to_broadcast([128, E, OWN0]),
                    ALU.is_ge))
                v.wait_ge(sv.h, sv.n)
                sv.inc(v.tensor_tensor(idv[:], msk[:], idxc[:], ALU.mult))
                v.wait_ge(sv.h, sv.n)
                sv.inc(v.tensor_scalar(idv[:], idv[:], -1.0, None, ALU.add))
            V(_)
            v_sel = sv.n

            def _(g):
                g.wait_ge(sv.h, v_sel)
                for c0 in range(8):
                    s_gath.inc(g.dma_start(
                        out=sgin[0:16, :, c0 * 16:(c0 + 1) * 16],
                        in_=idv[16 * c0:16 * (c0 + 1), :].rearrange("q (e m) -> q e m", e=E)), 16)
            G(_)
            sgin_done = s_gath.n
            p2.close()


        if phases >= 3:
            p3 = ExitStack()
            e3 = p3.enter_context
            xg = [e3(nc.sbuf_tensor(f"xgt{i}", [128, 8, CAP], BF16)) for i in range(2)]
            pg = [e3(nc.sbuf_tensor(f"pgt{i}", [128, NBLK, 64], F32)) for i in range(2)]
            hball = e3(nc.sbuf_tensor("hball", [128, 32, CAP], BF16))
            ysb = e3(nc.sbuf_tensor("ysb", [128, NBLK, C], F32))
            h_ps = [e3(nc.psum_tensor(f"hps{i}", [128, CAP], F32)) for i in range(2)]
            y2_ps = [e3(nc.psum_tensor(f"y2ps{i}", [128, C], F32)) for i in range(2)]

            wq_snap = {}
            sg_done = [0] * E
            gd_gather = [0] * E
            gd_scatter = [0] * E
            pe_h_all = {}
            pe_y2_all = {}
            v_yscale = [0] * E

            spe0, ssc0, sv0 = spe.n, ssc.n, sv.n

            def pe_h_p(e_, hk):
                return spe0 + 37 * e_ + hk + 1

            def pe_w1done_p(e_):
                return spe0 + 37 * e_ + 32

            def pe_y2done_p(e_):
                return spe0 + 37 * e_ + 37

            def sc_gelu_p(e_, hk):
                return ssc0 + 32 * e_ + hk + 1

            def sc_geludone_p(e_):
                return ssc0 + 32 * e_ + 32

            def v_ic_p(e_, ic):
                return sv0 + 5 * e_ + ic + 1

            def v_yscale_p(e_):
                return sv0 + 5 * e_ + 5

            def wload_w1(g, e_, q):
                gq = 4 * e_ + q - 2
                if gq >= 0:
                    g.wait_ge(spe.h, pe_h_p(gq // 4, 8 * (gq % 4) + 7))
                s_w1[q % 2].inc(g.dma_start(out=w1s[q % 2][:], in_=d_w1[e_, q]), 16)
                wq_snap[(e_, q)] = (s_w1[q % 2], s_w1[q % 2].n)

            def wload_w2(g, e_, q):
                if e_ > 0:
                    g.wait_ge(spe.h, pe_y2done_p(e_ - 1))
                s_w2.inc(g.dma_start(out=w2s[:, q * 8:(q + 1) * 8, :], in_=d_w2[e_, q]), 16)
                wq_snap[(e_, 4 + q)] = (s_w2, s_w2.n)

            for e_ in range(E):
                def _(g, e_=e_):
                    if e_ == 0:
                        g.wait_ge(sv.h, v_init)
                        g.wait_ge(sv.h, v_sel)
                        g.wait_ge(s_gath.h, sgin_done)
                    g.load_library(library_config.sparse_gather)
                    sgp.inc(g.sparse_gather(sgout[0:16, e_, :], sgin[0:16, e_, :],
                                            num_found=nfb[0:1, e_:e_ + 1]))
                    sg_done[e_] = sgp.n
                    g.wait_ge(sgp.h, sg_done[e_])
                    s_gath.inc(g.dma_start(out=idxall[0:16, e_, :],
                                           in_=sgout[0:16, e_, 0:CAP // 16]), 16)
                    s_gath.inc(g.dma_start(out=idxall[16:32, e_, :],
                                           in_=sgout[0:16, e_, 0:CAP // 16]), 16)
                    g.wait_ge(s_gath.h, s_gath.n)
                    g.load_library(library_config.mlp)
                    if e_ == 0:
                        for _sem, _val in xn_wr_snap:
                            g.wait_ge(_sem.h, _val)
                    if e_ >= 2:
                        g.wait_ge(spe.h, pe_w1done_p(e_ - 2))
                        g.wait_ge(sv.h, v_yscale_p(e_ - 2))
                    s_gath.inc(g.dma_gather(xg[e_ % 2][:], d_xn, idxall[:, e_, :], CAP, CAP, C,
                                            elem_step=C, transpose=True), 16)
                    s_gath.inc(g.dma_gather(pg[e_ % 2][:], d_pr, idxall[:, e_, :], CAP, CAP, 64,
                                            elem_step=64), 16)
                    gd_gather[e_] = s_gath.n
                    wload_w1(g, e_, 0)
                    wload_w1(g, e_, 1)
                    if e_ > 0:
                        g.wait_ge(sv.h, v_yscale_p(e_ - 1))
                        g.wait_ge(s_acc0.h, acc0_done)
                        if e_ > 1:
                            g.wait_ge(s_scat.h, gd_scatter[e_ - 2])
                        s_scat.inc(g.dma_scatter_add(d_acc, ysb[:], idxall[:, e_ - 1, :],
                                                     CAP, CAP, C, elem_step=C), 16)
                        gd_scatter[e_ - 1] = s_scat.n
                    wload_w1(g, e_, 2)
                    wload_w2(g, e_, 0)
                    wload_w2(g, e_, 1)
                    wload_w1(g, e_, 3)
                    wload_w2(g, e_, 2)
                    wload_w2(g, e_, 3)
                G(_)

                def _(t, e_=e_):
                    for hk in range(32):
                        q, hq = hk // 8, hk % 8
                        if hk == 0:
                            t.wait_ge(s_gath.h, gd_gather[e_])
                            if e_ == 0:
                                t.wait_ge(sv.h, v_sel)
                        _ws, _wv = wq_snap[(e_, q)]
                        t.wait_ge(_ws.h, _wv)
                        if hk >= 2:
                            t.wait_ge(ssc.h, sc_gelu_p(e_, hk - 2))
                        elif e_ > 0:
                            t.wait_ge(ssc.h, sc_geludone_p(e_ - 1))
                        ps = h_ps[hk % 2]
                        for cc in range(8):
                            t.matmul(ps[:, 0:512], w1s[q % 2][:, cc, hq * 128:(hq + 1) * 128],
                                     xg[e_ % 2][:, cc, 0:512], start=(cc == 0), stop=(cc == 7))
                            mm = t.matmul(ps[:, 512:CAP], w1s[q % 2][:, cc, hq * 128:(hq + 1) * 128],
                                          xg[e_ % 2][:, cc, 512:CAP], start=(cc == 0), stop=(cc == 7))
                        assert spe.inc(mm) == pe_h_p(e_, hk)
                        pe_h_all[(e_, hk)] = spe.n
                    for ic in range(NBLK):
                        if ic == 0:
                            t.wait_ge(ssc.h, sc_gelu_p(e_, 29))
                            _ws, _wv = wq_snap[(e_, 4 + 3)]
                            t.wait_ge(_ws.h, _wv)
                            if e_ > 0:
                                t.wait_ge(sv.h, v_yscale_p(e_ - 1))
                        if ic >= 2:
                            t.wait_ge(sv.h, v_ic_p(e_, ic - 2))
                        ps = y2_ps[ic % 2]
                        for hk in range(32):
                            if ic == 0 and hk >= 30:
                                t.wait_ge(ssc.h, sc_gelu_p(e_, hk))
                            t.matmul(ps[:, 0:512], hball[:, hk, ic * 128:(ic + 1) * 128],
                                     w2s[:, hk, 0:512], start=(hk == 0), stop=(hk == 31))
                            mm = t.matmul(ps[:, 512:1024], hball[:, hk, ic * 128:(ic + 1) * 128],
                                          w2s[:, hk, 512:1024], start=(hk == 0), stop=(hk == 31))
                        assert spe.inc(mm) == spe0 + 37 * e_ + 32 + ic + 1
                        pe_y2_all[(e_, ic)] = spe.n
                PE(_)

                def _(v, e_=e_):
                    for ic in range(NBLK):
                        v.wait_ge(spe.h, pe_y2_all[(e_, ic)])
                        if ic == 0:
                            v.wait_ge(s_gath.h, gd_gather[e_])
                            if e_ > 0:
                                v.wait_ge(s_scat.h, gd_scatter[e_ - 1])
                        _n = sv.inc(v.tensor_scalar(ysb[:, ic, :], y2_ps[ic % 2][:],
                                                    pg[e_ % 2][:, ic, e_:e_ + 1], None, ALU.mult))
                        assert _n == v_ic_p(e_, ic)
                V(_)
                v_yscale[e_] = sv.n
                assert v_yscale[e_] == v_yscale_p(e_)

            def _(s):
                for e_ in range(E):
                    for hk in range(32):
                        s.wait_ge(spe.h, pe_h_all[(e_, hk)])
                        if e_ > 0 and hk == 0:
                            s.wait_ge(spe.h, pe_y2done_p(e_ - 1))
                        _n = ssc.inc(s.activation(hball[:, hk, :], h_ps[hk % 2][:], AF.Gelu))
                        assert _n == sc_gelu_p(e_, hk)
            SC(_)

            def _(g):
                g.wait_ge(sv.h, v_yscale[E - 1])
                g.wait_ge(s_scat.h, gd_scatter[E - 2])
                s_scat.inc(g.dma_scatter_add(d_acc, ysb[:], idxall[:, E - 1, :],
                                             CAP, CAP, C, elem_step=C), 16)
                gd_scatter[E - 1] = s_scat.n
            G(_)
            all_scattered = s_scat.n
            p3.close()


        if phases >= 4:
            p4 = ExitStack()
            e4 = p4.enter_context
            TDEP = 4
            acct = [e4(nc.sbuf_tensor(f"acct{i}", [128, C], F32)) for i in range(TDEP)]
            gt = [e4(nc.sbuf_tensor(f"gtt{i}", [128, C], F32)) for i in range(TDEP)]
            stl = [e4(nc.sbuf_tensor(f"stl{i}", [128, C], F32)) for i in range(TDEP)]
            xt = [e4(nc.sbuf_tensor(f"xtt{i}", [128, C], F32)) for i in range(TDEP)]
            ot = [e4(nc.sbuf_tensor(f"ott{i}", [128, C], F32)) for i in range(TDEP)]

            tail_ta = [0] * 16
            tail_tg = [0] * 16
            tail_sc = [0] * 16
            tail_v = [0] * 16
            tail_st = [0] * 16
            for k in range(16):
                par = k % TDEP
                r0 = k * 128

                def _(sync, k=k, par=par, r0=r0):
                    if k == 0:
                        sync.wait_ge(s_scat.h, all_scattered)
                    if k >= TDEP:
                        sync.wait_ge(ssc.h, tail_sc[k - TDEP])
                    s_ta[par].inc(sync.dma_start(out=acct[par][:], in_=d_acc[r0:r0 + 128, :]), 16)
                    if k % 2 == 0:
                        if k >= TDEP:
                            sync.wait_ge(sv.h, tail_v[k - TDEP])
                        s_ta[par].inc(sync.dma_start(
                            out=xt[par][:], in_=d_z[HALF + r0:HALF + r0 + 128, :]), 16)
                    tail_ta[k] = s_ta[par].n
                SY(_)

                def _(g, k=k, par=par, r0=r0):
                    if k == 0:
                        g.wait_ge(s_scat.h, all_scattered)
                    if k >= TDEP:
                        g.wait_ge(sv.h, tail_v[k - TDEP])
                    s_tg[par].inc(g.dma_start(out=stl[par][:], in_=d_s[r0:r0 + 128, :]), 16)
                    if k % 2 == 1:
                        s_tg[par].inc(g.dma_start(
                            out=xt[par][:], in_=d_z[HALF + r0:HALF + r0 + 128, :]), 16)
                    tail_tg[k] = s_tg[par].n
                G(_)

                def _(s, k=k, par=par):
                    if k == 0:
                        s.wait_ge(s_scat.h, all_scattered)
                    s.wait_ge(s_ta[par].h, tail_ta[k])
                    s.wait_ge(s_tg[par].h, tail_tg[k])
                    if k >= TDEP:
                        s.wait_ge(sv.h, tail_v[k - TDEP])
                    ssc.inc(s.activation(gt[par][:], acct[par][:], AF.Silu))
                    if k >= 1:
                        s.wait_ge(sv.h, tail_v[k - 1])
                        s_st[(k - 1) % TDEP].inc(s.dma_start(
                            out=d_out[(k - 1) * 128:k * 128, :], in_=ot[(k - 1) % TDEP][:]), 16)
                        tail_st[k - 1] = s_st[(k - 1) % TDEP].n
                SC(_)
                tail_sc[k] = ssc.n

                def _(v, k=k, par=par):
                    if k == 0:
                        v.wait_ge(s_scat.h, all_scattered)
                    v.wait_ge(ssc.h, tail_sc[k])
                    v.wait_ge(s_ta[par].h, tail_ta[k])
                    v.wait_ge(s_tg[par].h, tail_tg[k])
                    if k >= TDEP:
                        v.wait_ge(s_st[par].h, tail_st[k - TDEP])
                    sv.inc(v.tensor_tensor(ot[par][:], gt[par][:], stl[par][:], ALU.mult))
                    v.wait_ge(sv.h, sv.n)
                    sv.inc(v.tensor_tensor(ot[par][:], ot[par][:], xt[par][:], ALU.add))
                V(_)
                tail_v[k] = sv.n

            def _(s, k=15):
                s.wait_ge(sv.h, tail_v[k])
                s_st[k % TDEP].inc(s.dma_start(
                    out=d_out[k * 128:(k + 1) * 128, :], in_=ot[k % TDEP][:]), 16)
                tail_st[k] = s_st[k % TDEP].n
            SC(_)

            def _(sync):
                for _s in s_st:
                    sync.wait_ge(_s.h, _s.n)
                for _s in s_tg:
                    sync.wait_ge(_s.h, _s.n)
            SY(_)
            p4.close()



        if phases == 1:
            def _(sync):
                for sem in _allsems:
                    sync.wait_ge(sem.h, sem.n)
                if dbg:
                    s_st[0].inc(sync.dma_start(out=d_dbgprobs, in_=probs[:]), 16)
                    s_st[0].inc(sync.dma_start(out=d_dbgu, in_=Ub[:]), 16)
                    s_st[0].inc(sync.dma_start(out=d_dbga, in_=apow[:]), 16)
                sync.wait_ge(s_st[0].h, s_st[0].n)
            SY(_)
        elif phases == 2:
            def _(g):
                g.wait_ge(sv.h, v_sel)
                g.wait_ge(s_gath.h, sgin_done)
                g.load_library(library_config.sparse_gather)
                for e_ in range(E):
                    sgp.inc(g.sparse_gather(sgout[0:16, e_, :], sgin[0:16, e_, :],
                                            num_found=nfb[0:1, e_:e_ + 1]))
                g.wait_ge(sgp.h, sgp.n)
                s_gath.inc(g.dma_start(out=idxall[0:16, :, :],
                                       in_=sgout[0:16, :, 0:CAP // 16]), 16)
                g.wait_ge(s_gath.h, s_gath.n)
                s_gath.inc(g.dma_start(out=idxall[16:32, :, :], in_=idxall[0:16, :, :]), 16)
                g.wait_ge(s_gath.h, s_gath.n)
            G(_)
            sel_all = s_gath.n

            def _(v):
                v.wait_ge(s_gath.h, sel_all)
                v.wait_ge(sgp.h, sgp.n)
                sv.inc(v.tensor_copy(bdbg[:, 0:E], probs[:, 0:E]))
            V(_)

            def _(sync):
                sync.wait_ge(s_gath.h, sel_all)
                sync.wait_ge(sgp.h, sgp.n)
                sync.wait_ge(sv.h, sv.n)
                if dbg:
                    s_st[0].inc(sync.dma_start(out=d_dbgprobs, in_=probs[:]), 16)
                    s_st[0].inc(sync.dma_start(out=d_dbgtau, in_=taub[:]), 16)
                    s_st[0].inc(sync.dma_start(out=d_dbgidx, in_=idxall[:]), 16)
                    s_st[0].inc(sync.dma_start(out=d_dbgnf, in_=nfb[:]), 16)
                    s_st[0].inc(sync.dma_start(out=d_dbgbi, in_=bdbg[:]), 16)
                sync.wait_ge(s_st[0].h, s_st[0].n)
            SY(_)
        elif phases == 3:
            def _(sync):
                sync.wait_ge(s_scat.h, all_scattered)
                if dbg:
                    s_st[0].inc(sync.dma_start(out=d_dbgprobs, in_=probs[:]), 16)
                    s_st[0].inc(sync.dma_start(out=d_dbgtau, in_=taub[:]), 16)
                    s_st[0].inc(sync.dma_start(out=d_dbgidx, in_=idxall[:]), 16)
                    s_st[0].inc(sync.dma_start(out=d_dbgnf, in_=nfb[:]), 16)
                sync.wait_ge(s_st[0].h, s_st[0].n)
            SY(_)

    mybir.codegen_inst_isa_subclasses(nc)
    return nc


def _host_consts():
    j = np.arange(128)[:, None].astype(np.float32)
    t = np.arange(128)[None, :].astype(np.float32)
    expn = np.where(t >= j, t - j, np.float32(1e30)).astype(np.float32)
    maskc = (t >= j).astype(np.float32)
    tp1 = np.broadcast_to(t + 1.0, (128, 128)).astype(np.float32).copy()
    p = np.arange(128)[:, None]
    em = np.arange(128)[None, :]
    idxc = ((em % 16) * 128 + p + 1).astype(np.float32)
    return expn, maskc, tp1, idxc


_PROG_CACHE = {}


def kernel(L0_attn_Z, L0_attn_decay, L0_attn_W, L0_attn_b, L0_attn_gamma,
           L0_attn_beta, L0_ffn_gate_w, L0_ffn_w1, L0_ffn_w2):
    Z = np.ascontiguousarray(np.asarray(L0_attn_Z, dtype=np.float32))
    decay = np.asarray(L0_attn_decay, dtype=np.float32).reshape(1, 1)
    W = np.ascontiguousarray(np.asarray(L0_attn_W, dtype=np.float32))
    bat = np.ascontiguousarray(np.asarray(L0_attn_b, dtype=np.float32).reshape(1, C))
    gam = np.ascontiguousarray(np.asarray(L0_attn_gamma, dtype=np.float32).reshape(1, C))
    bet = np.ascontiguousarray(np.asarray(L0_attn_beta, dtype=np.float32).reshape(1, C))
    gw = np.ascontiguousarray(np.asarray(L0_ffn_gate_w, dtype=np.float32))
    w1 = np.asarray(L0_ffn_w1, dtype=np.float32)
    w2 = np.asarray(L0_ffn_w2, dtype=np.float32)

    w1r = np.ascontiguousarray(
        w1.reshape(E, 8, 128, 4, 1024).transpose(0, 3, 2, 1, 4))
    w2r = np.ascontiguousarray(
        w2.reshape(E, 4, 8, 128, 1024).transpose(0, 1, 3, 2, 4))
    W_t = np.ascontiguousarray(W.T)
    expn, maskc, tp1, idxc = _host_consts()

    if "nc" not in _PROG_CACHE:
        _PROG_CACHE["nc"] = build_program()
    nc = _PROG_CACHE["nc"]

    in_maps = []
    for core in range(8):
        b, h = core // 2, core % 2
        if h == 0:
            zc = np.ascontiguousarray(np.concatenate([Z[b, HALF:], Z[b, :HALF]], axis=0))
            kapv = 0.0
        else:
            zc = np.ascontiguousarray(Z[b])
            kapv = 1.0
        in_maps.append(dict(
            zc=zc,
            zct=np.ascontiguousarray(zc.T),
            kap=np.full((128, 1), kapv, np.float32),
            dec=decay,
            gw=gw, wt=W_t, batt=bat, gam=gam, bet=bet,
            w1r=w1r, w2r=w2r,
            expn=expn, maskc=maskc, tp1=tp1, idxc=idxc,
        ))

    res = run_bass_kernel_spmd(nc, in_maps, core_ids=list(range(8)))
    out = np.empty((B, T, C), np.float32)
    for core in range(8):
        b, h = core // 2, core % 2
        shard = np.asarray(res.results[core]["out"], dtype=np.float32)
        out[b, h * HALF:(h + 1) * HALF] = shard
    return out
